# version 09
# baseline (speedup 1.0000x reference)
import numpy as np
import ml_dtypes
import concourse.bass as bass
import concourse.mybir as mybir

F32 = mybir.dt.float32
BF16 = mybir.dt.bfloat16
I32 = mybir.dt.int32
AF = mybir.ActivationFunctionType
ALU = mybir.AluOpType
AX = mybir.AxisListType

R = 1024
D = 4096
NI = 2048
EPS = 1e-5
LN2C = float(np.log(2.0))


def host_prep(inputs):
    u = np.asarray(inputs["L0_attn_u"], np.float32).reshape(8192, 4096)
    wg = np.asarray(inputs["L0_ffn_w_gate"], np.float32)
    wu = np.asarray(inputs["L0_ffn_w_up"], np.float32)
    wd = np.asarray(inputs["L0_ffn_w_down"], np.float32)
    wgt = np.ascontiguousarray(wg.reshape(16, 128, 32, 128).transpose(0, 3, 2, 1))
    wut = np.ascontiguousarray(wu.reshape(16, 128, 32, 128).transpose(0, 3, 2, 1))
    wdt = np.ascontiguousarray(wd.reshape(8, 512, 16, 128).transpose(0, 3, 2, 1))
    shared = {
        "wg_t": wgt, "wu_t": wut, "wd_t": wdt,
        "s_wg": np.ascontiguousarray(np.asarray(inputs["L0_ffn_s_wgate"], np.float32).reshape(16, 128).T),
        "s_wu": np.ascontiguousarray(np.asarray(inputs["L0_ffn_s_wup"], np.float32).reshape(16, 128).T),
        "s_wd": np.asarray(inputs["L0_ffn_s_wdown"], np.float32).reshape(1, 4096),
        "inv_g": np.ascontiguousarray(np.asarray(inputs["L0_ffn_inv_gate"], np.float32).reshape(32, 128).T),
        "inv_u": np.ascontiguousarray(np.asarray(inputs["L0_ffn_inv_up"], np.float32).reshape(32, 128).T),
        "inv_i": np.ascontiguousarray(np.asarray(inputs["L0_ffn_inv_inter"], np.float32).reshape(16, 128).T),
        "lam_row": np.asarray(inputs["L0_attn_Lambda_ri"], np.float32).reshape(1, 128),
        "p_row": np.asarray(inputs["L0_attn_p_ri"], np.float32).reshape(1, 128),
        "q_row": np.asarray(inputs["L0_attn_q_ri"], np.float32).reshape(1, 128),
        "ct_row": np.asarray(inputs["L0_attn_Ct_ri"], np.float32).reshape(1, 128),
        "vc_nat": np.asarray(inputs["L0_attn_Vc_ri"], np.float32).reshape(64, 128),
        "bp_col": np.asarray(inputs["L0_attn_Bp"], np.float32),
        "dd": np.asarray(inputs["L0_attn_D"], np.float32).reshape(1, 1),
        "log_step": np.asarray(inputs["L0_attn_log_step"], np.float32).reshape(1, 1),
    }
    per_core = []
    for k in range(8):
        m = dict(shared)
        m["x_in"] = np.ascontiguousarray(u[k * R:(k + 1) * R])
        per_core.append(m)
    return per_core


def consts():
    c = {}
    c["ident16"] = np.eye(128, dtype=ml_dtypes.bfloat16)
    c["nident16"] = (-np.eye(128)).astype(ml_dtypes.bfloat16)
    c["ident32"] = np.eye(128, dtype=np.float32)
    c["ones_row"] = np.ones((1, 128), np.float32)
    p_idx = np.arange(128)[:, None]
    c_idx = np.arange(32)[None, :]
    jtab = (c_idx + 32 * (p_idx >= 64)) + 64 * (p_idx % 64)
    Om = np.exp(-2j * np.pi * (np.arange(4096) / 4096.0))
    T1 = (1.0 - Om) / (1.0 + Om)
    cc = 2.0 / (1.0 + Om)
    c["t1r"] = T1.real[jtab].astype(np.float32)
    c["t1i"] = T1.imag[jtab].astype(np.float32)
    c["ccr"] = cc.real[jtab].astype(np.float32)
    c["cci"] = cc.imag[jtab].astype(np.float32)
    F64 = np.exp(-2j * np.pi * np.outer(np.arange(64), np.arange(64)) / 64.0)
    c["f64c"] = np.vstack([F64.real, F64.real]).astype(np.float32)
    c["f64s"] = np.vstack([F64.imag, F64.imag]).astype(np.float32)
    TW = np.exp(-2j * np.pi * np.outer(np.arange(64), np.arange(64)) / 4096.0)
    c["twc"] = TW.real.astype(np.float32)
    c["tws"] = TW.imag.astype(np.float32)
    c["f64sn"] = np.vstack([-F64.imag, -F64.imag]).astype(np.float32)
    c["f64c2"] = (F64.real / 4096.0).astype(np.float32)
    c["f64s2n"] = (-F64.imag / 4096.0).astype(np.float32)
    return c


class MK:

    def __init__(self, nc):
        self.nc = nc
        self.sem = {}
        self.cnt = {}

    def mksem(self, stack, *names):
        for n in names:
            self.sem[n] = stack.enter_context(self.nc.semaphore(n))
            self.cnt[n] = 0

    def inc(self, instr, name, amt):
        instr.then_inc(self.sem[name], amt)
        self.cnt[name] += amt
        return self.cnt[name]

    def tok(self, name):
        return self.cnt[name]

    def wait(self, eng, name, tok=None):
        eng.wait_ge(self.sem[name], self.cnt[name] if tok is None else tok)

    def guard(self, eng, name="tv"):
        ins = eng.engine_nop() if hasattr(eng, "engine_nop") else eng.nop()
        t = self.inc(ins, name, 1)
        eng.wait_ge(self.sem[name], t)


def build(n_ffn_blocks=2, do_ssm=True, dbg=False):
    nc = bass.Bass("TRN2", target_bir_lowering=False)
    mk = MK(nc)
    CST = consts()

    x_in = nc.dram_tensor("x_in", [R, D], F32, kind="ExternalInput").ap()
    wg_t = nc.dram_tensor("wg_t", [16, 128, 32, 128], F32, kind="ExternalInput").ap()
    wu_t = nc.dram_tensor("wu_t", [16, 128, 32, 128], F32, kind="ExternalInput").ap()
    wd_t = nc.dram_tensor("wd_t", [8, 128, 16, 512], F32, kind="ExternalInput").ap()
    s_wg = nc.dram_tensor("s_wg", [128, 16], F32, kind="ExternalInput").ap()
    s_wu = nc.dram_tensor("s_wu", [128, 16], F32, kind="ExternalInput").ap()
    s_wd = nc.dram_tensor("s_wd", [1, 4096], F32, kind="ExternalInput").ap()
    inv_g = nc.dram_tensor("inv_g", [128, 32], F32, kind="ExternalInput").ap()
    inv_u = nc.dram_tensor("inv_u", [128, 32], F32, kind="ExternalInput").ap()
    inv_i = nc.dram_tensor("inv_i", [128, 16], F32, kind="ExternalInput").ap()
    lam_row = nc.dram_tensor("lam_row", [1, 128], F32, kind="ExternalInput").ap()
    p_row = nc.dram_tensor("p_row", [1, 128], F32, kind="ExternalInput").ap()
    q_row = nc.dram_tensor("q_row", [1, 128], F32, kind="ExternalInput").ap()
    ct_row = nc.dram_tensor("ct_row", [1, 128], F32, kind="ExternalInput").ap()
    vc_nat = nc.dram_tensor("vc_nat", [64, 128], F32, kind="ExternalInput").ap()
    bp_col = nc.dram_tensor("bp_col", [64, 1], F32, kind="ExternalInput").ap()
    dd_in = nc.dram_tensor("dd", [1, 1], F32, kind="ExternalInput").ap()
    log_step = nc.dram_tensor("log_step", [1, 1], F32, kind="ExternalInput").ap()
    out = nc.dram_tensor("out", [R, D], F32, kind="ExternalOutput").ap()
    kindi = "ExternalOutput" if dbg else "Internal"

    x_work = nc.dram_tensor("x_work", [R, D], F32, kind=kindi).ap()
    bounce = nc.dram_tensor("bounce", [R, D], BF16, kind=kindi).ap()
    sg_park = nc.dram_tensor("sg_park", [NI, R], BF16, kind=kindi).ap()
    buf3 = nc.dram_tensor("buf3", [1, 8192], BF16, kind=kindi).ap()
    row_sc = nc.dram_tensor("row_sc", [1, 8192], F32, kind=kindi).ap()
    dbg_qg = nc.dram_tensor("dbg_qg", [D, R], BF16, kind=kindi).ap() if dbg else None
    dbg_it = nc.dram_tensor("dbg_it", [NI, R], BF16, kind=kindi).ap() if dbg else None
    dbg_st = nc.dram_tensor("dbg_st", [128, 64], F32, kind=kindi).ap() if dbg else None
    dbg_q2 = nc.dram_tensor("dbg_q2", [128, 3072], BF16, kind=kindi).ap() if dbg else None
    dbg_q3 = nc.dram_tensor("dbg_q3", [128, 48], F32, kind=kindi).ap() if dbg else None

    ic16 = nc.inline_tensor(CST["ident16"], "ic16").ap()
    ic32 = nc.inline_tensor(CST["ident32"], "ic32").ap()
    ones_row = nc.inline_tensor(CST["ones_row"], "ones_row").ap()
    t1r_c = nc.inline_tensor(CST["t1r"], "t1r").ap()
    t1i_c = nc.inline_tensor(CST["t1i"], "t1i").ap()
    ccr_c = nc.inline_tensor(CST["ccr"], "ccr").ap()
    cci_c = nc.inline_tensor(CST["cci"], "cci").ap()
    f64c_c = nc.inline_tensor(CST["f64c"], "f64c").ap()
    f64s_c = nc.inline_tensor(CST["f64s"], "f64s").ap()
    f64sn_c = nc.inline_tensor(CST["f64sn"], "f64sn").ap()
    twc_c = nc.inline_tensor(CST["twc"], "twc").ap()
    tws_c = nc.inline_tensor(CST["tws"], "tws").ap()
    f64c2_c = nc.inline_tensor(CST["f64c2"], "f64c2").ap()
    f64s2n_c = nc.inline_tensor(CST["f64s2n"], "f64s2n").ap()

    from contextlib import ExitStack
    with ExitStack() as stack:
        sbt = lambda n, s, d: stack.enter_context(nc.sbuf_tensor(n, s, d))
        arena = sbt("arena", [128, 92176], BF16)
        AX_, AG, AT = 0, 32768, 65536
        ident16 = sbt("ident16", [128, 128], BF16).ap()
        ident32 = sbt("ident32", [128, 128], F32).ap()
        onesr = sbt("onesr", [1, 128], F32).ap()
        s_invg = sbt("s_invg", [128, 32], F32)
        s_invu = sbt("s_invu", [128, 32], F32)
        s_invi = sbt("s_invi", [128, 16], F32)
        s_swg = sbt("s_swg", [128, 16], F32)
        s_swu = sbt("s_swu", [128, 16], F32)
        stats = sbt("stats", [128, 80], F32)
        pS = sbt("pS", [128, 64], F32)
        pQ = sbt("pQ", [128, 64], F32)
        qsc = sbt("qsc", [128, 48], F32)
        qtmp = sbt("qtmp", [128, 6144], BF16)
        qrow = sbt("qrow", [16, 128], F32)
        wslab = sbt("wslab", [128, 2, 2048], BF16)
        ktile = sbt("ktile", [64, 64], BF16)
        scB = sbt("scB", [128, 1024], F32)

        pbig0 = stack.enter_context(nc.psum_tensor("pbig0", [128, 1024], F32))
        pbig1 = stack.enter_context(nc.psum_tensor("pbig1", [128, 1024], F32))
        p512 = [stack.enter_context(nc.psum_tensor(f"p512_{i}", [128, 512], F32)) for i in range(4)]

        mk.mksem(stack, "sd0", "sd1", "sw0", "sw1", "sq", "sgq", "tv", "tsc", "tt", "sdo")
        block = stack.enter_context(nc.Block())

        def av(base, off, n, dtype=BF16, parts=128, p0=0):
            if dtype == BF16:
                ap = arena[p0:p0 + parts, base + off:base + off + n]
            elif dtype in (F32, I32):
                ap = arena[p0:p0 + parts, base + off:base + off + 2 * n].bitcast(dtype)
            else:
                raise ValueError(dtype)
            return ap

        @block.sync
        def _(sync):
            for dst, src in [(ident16[:], ic16), (ident32[:], ic32), (onesr[:], ones_row),
                             (s_invg[:], inv_g), (s_invu[:], inv_u), (s_invi[:], inv_i),
                             (s_swg[:], s_wg), (s_swu[:], s_wu)]:
                mk.inc(sync.dma_start(out=dst, in_=src), "sdo", 16)

        init_tok = mk.tok("sdo")

        kenv = dict(
            av=av, AG=AG, AT=AT, onesr=onesr, ident32=ident32, ktile=ktile,
            pbig0=pbig0, pbig1=pbig1, p512=p512, buf3=buf3,
            lam_row=lam_row, p_row=p_row, q_row=q_row, ct_row=ct_row,
            vc_nat=vc_nat, bp_col=bp_col, log_step=log_step,
            t1r=t1r_c, t1i=t1i_c, ccr=ccr_c, cci=cci_c,
            f64c=f64c_c, f64s=f64s_c, f64sn=f64sn_c, twc=twc_c, tws=tws_c,
            f64c2=f64c2_c, f64s2n=f64s2n_c, init_tok=init_tok)
        if do_ssm:
            KL = k_layout(kenv)
            phase_k_a(mk, block, kenv, KL)
            hooks = dict(
                hook_after_gate=lambda: phase_k_b(mk, block, kenv, KL),
                hook_after_up=lambda: phase_k_c(mk, block, kenv, KL, None))
        else:
            hooks = dict()

        env = dict(av=av, AX=AX_, AG=AG, AT=AT, arena=arena,
                   onesr=onesr, ident16=ident16, ident32=ident32,
                   s_invg=s_invg, s_invu=s_invu, s_invi=s_invi,
                   s_swg=s_swg, s_swu=s_swu, s_wd=s_wd,
                   stats=stats, pS=pS, pQ=pQ, qsc=qsc, qtmp=qtmp, qrow=qrow, wslab=wslab,
                   scB=scB,
                   pbig0=pbig0, pbig1=pbig1, p512=p512,
                   wg_t=wg_t, wu_t=wu_t, wd_t=wd_t,
                   bounce=bounce, sg_park=sg_park, row_sc=row_sc,
                   buf3=buf3, dd_in=dd_in, ktile=ktile,
                   dbg_qg=dbg_qg, dbg_it=dbg_it, dbg_st=dbg_st, dbg_q2=dbg_q2, dbg_q3=dbg_q3, dbg=dbg)

        if n_ffn_blocks >= 1:
            phase_ffn(mk, block, env, src=x_in,
                      dst=(x_work if (do_ssm or n_ffn_blocks >= 2) else out),
                      collect_stats=do_ssm, **hooks)
        if do_ssm:
            phase_ssm(mk, block, env, src=x_work, dst=(x_work if n_ffn_blocks >= 2 else out),
                      stats_in=True, collect_stats=(n_ffn_blocks >= 2))
        if n_ffn_blocks >= 2:
            phase_ffn(mk, block, env, src=x_work, dst=out, stats_in=True)

        @block.sync
        def _(sync):
            mk.wait(sync, "sdo")
            mk.wait(sync, "sd0")
            mk.wait(sync, "sd1")

    return nc


def k_layout(e):
    av, AG, AT = e["av"], e["AG"], e["AT"]
    KR = lambda off, n, parts=128, p0=0: av(AG, 2 * off, n, F32, parts, p0)
    TTL = lambda off, n, parts=128, p0=0: av(AT, 2 * off, n, F32, parts, p0)
    L = {}
    B0 = 12288
    L["vB"] = [TTL(B0 + 64 * i, 64) for i in range(8)]
    L["nLr"] = TTL(B0 + 512, 64); L["nLi"] = TTL(B0 + 576, 64)
    L["gr"] = TTL(B0 + 640, 32); L["gi"] = TTL(B0 + 672, 32)
    L["t1r"] = TTL(B0 + 704, 32); L["t1i"] = TTL(B0 + 736, 32)
    L["ccr"] = TTL(B0 + 768, 32); L["cci"] = TTL(B0 + 800, 32)
    L["s2b"] = TTL(B0 + 832, 1)
    L["f64c"] = TTL(B0 + 840, 64, 128)
    L["f64s"] = TTL(B0 + 904, 64, 128)
    L["f64sn"] = TTL(B0 + 968, 64, 128)
    L["twc"] = KR(15872, 64, 64)
    L["tws"] = KR(15936, 64, 64)
    L["f64c2"] = KR(16000, 64, 64)
    L["f64s2n"] = KR(16064, 64, 64)
    L["lamr"] = KR(0, 128, 1, 0); L["pr"] = KR(128, 128, 1, 0)
    L["qr"] = KR(256, 128, 1, 0); L["ctr"] = KR(384, 128, 1, 0)
    L["Brow"] = KR(512, 64, 1, 0); L["Birow"] = KR(576, 64, 1, 0)
    L["vrow"] = [KR(640 + 64 * i, 64, 1, 0) for i in range(8)]
    L["tA"] = KR(1152, 64, 1, 0); L["tB"] = KR(1216, 64, 1, 0)
    L["lsrow"] = KR(1280, 1, 1, 0); L["s2row"] = KR(1281, 1, 1, 0)
    L["s2tmp"] = KR(1284, 1, 1, 0)
    L["vcn"] = KR(1536, 128, 64)
    L["vcTr"] = KR(1664, 64, 64); L["vcTi"] = KR(1728, 64, 64)
    L["vcr_s"] = KR(1792, 64, 64); L["vci_s"] = KR(1856, 64, 64)
    L["bp"] = KR(1920, 1, 64)
    L["kx"] = [KR(14336 + 32 * i, 32) for i in range(8)]
    L["arre"] = KR(14592, 32); L["aim"] = KR(14624, 32)
    L["t1"] = KR(14656, 64); L["t2"] = KR(14720, 64)
    L["tmp"] = KR(14784, 32); L["tmp2"] = KR(14816, 32); L["tmp3"] = KR(14848, 32)
    L["arre2"] = KR(14880, 32); L["aim2"] = KR(14912, 32)
    L["KR3"] = lambda off: e["av"](e["AG"], 2 * off, 2048, F32).rearrange("p (c n) -> p c n", n=64)
    L["srT"] = KR(1024 - 1024 + 15360, 64, 64) if False else None
    L["srT"] = KR(15360, 64, 64); L["siT"] = KR(15424, 64, 64)
    L["spr"] = KR(15488, 64, 64); L["spi"] = KR(15552, 64, 64)
    L["sprT"] = KR(15616, 64, 64); L["spiT"] = KR(15680, 64, 64)
    L["tw1"] = KR(15744, 64, 64); L["tw2"] = KR(15808, 64, 64)
    return L


def phase_k_a(mk, block, e, L):
    nc = mk.nc
    p512 = e["p512"]
    ident32, onesr = e["ident32"], e["onesr"]

    @block.sync
    def _(sync):
        mk.wait(sync, "sdo", e["init_tok"])
        for dst, srcc in [(L["lamr"], e["lam_row"]), (L["pr"], e["p_row"]), (L["qr"], e["q_row"]),
                          (L["ctr"], e["ct_row"]), (L["vcn"], e["vc_nat"]), (L["bp"], e["bp_col"]),
                          (L["lsrow"], e["log_step"]),
                          (L["t1r"], e["t1r"]), (L["t1i"], e["t1i"]),
                          (L["ccr"], e["ccr"]), (L["cci"], e["cci"]),
                          (L["f64c"], e["f64c"]), (L["f64s"], e["f64s"]), (L["f64sn"], e["f64sn"])]:
            mk.inc(sync.dma_start(out=dst, in_=srcc), "sd0", 16)
    k_loads = mk.tok("sd0")

    @block.vector
    def _(vector):
        mk.wait(vector, "sd0", k_loads)
        vector.tensor_copy(out=L["vcr_s"], in_=L["vcn"][:, 0:128:2])
        vector.tensor_copy(out=L["vci_s"], in_=L["vcn"][:, 1:128:2])
        mk.inc(vector.tensor_scalar(out=L["s2tmp"], in0=L["lsrow"], scalar1=-1.0, scalar2=LN2C,
                                    op0=ALU.mult, op1=ALU.add), "tv", 1)
    t_split = mk.tok("tv")

    @block.scalar
    def _(scalar):
        mk.wait(scalar, "tv", t_split)
        mk.inc(scalar.activation(out=L["s2row"], in_=L["s2tmp"], func=AF.Exp), "tsc", 1)
    t_s2 = mk.tok("tsc")

    @block.tensor
    def _(tensor):
        mk.wait(tensor, "tv", t_split)
        mk.inc(tensor.transpose(p512[0][0:64, 0:64], L["vcr_s"], ident32[0:64, 0:64]), "tt", 1)
        mk.inc(tensor.transpose(p512[1][0:64, 0:64], L["vci_s"], ident32[0:64, 0:64]), "tt", 1)
    t_vct = mk.tok("tt")

    @block.vector
    def _(vector):
        mk.wait(vector, "tt", t_vct)
        vector.tensor_copy(out=L["vcTr"], in_=p512[0][0:64, 0:64])
        vector.tensor_copy(out=L["vcTi"], in_=p512[1][0:64, 0:64])
        mk.inc(vector.engine_nop(), "tv", 1)
    t_vce = mk.tok("tv")

    @block.tensor
    def _(tensor):
        mk.wait(tensor, "tv", t_vce)
        mk.inc(tensor.matmul(p512[0][0:1, 0:64], L["bp"], L["vcTr"], start=True, stop=True), "tt", 1)
        mk.inc(tensor.matmul(p512[1][0:1, 0:64], L["bp"], L["vcTi"], start=True, stop=True), "tt", 1)
    t_b = mk.tok("tt")

    @block.vector
    def _(vector):
        mk.wait(vector, "tt", t_b)
        vector.tensor_copy(out=L["Brow"], in_=p512[0][0:1, 0:64])
        vector.tensor_copy(out=L["Birow"], in_=p512[1][0:1, 0:64])
        mk.guard(vector)
        ctr = L["ctr"][0:1, 0:128:2]; cti = L["ctr"][0:1, 1:128:2]
        pr = L["pr"][0:1, 0:128:2]; pi = L["pr"][0:1, 1:128:2]
        qr = L["qr"][0:1, 0:128:2]; qi = L["qr"][0:1, 1:128:2]
        tA, tB = L["tA"], L["tB"]
        vrow = L["vrow"]

        def cmul_conj(vr_out, vi_out, xr, xi, yr, yi):
            vector.tensor_tensor(out=tA, in0=xr, in1=yr, op=ALU.mult)
            vector.tensor_tensor(out=tB, in0=xi, in1=yi, op=ALU.mult)
            mk.guard(vector)
            vector.tensor_tensor(out=vr_out, in0=tA, in1=tB, op=ALU.add)
            vector.tensor_tensor(out=tA, in0=xr, in1=yi, op=ALU.mult)
            vector.tensor_tensor(out=tB, in0=xi, in1=yr, op=ALU.mult)
            mk.guard(vector)
            vector.tensor_tensor(out=vi_out, in0=tA, in1=tB, op=ALU.subtract)
        cmul_conj(vrow[0], vrow[1], ctr, cti, L["Brow"], L["Birow"])
        cmul_conj(vrow[2], vrow[3], ctr, cti, pr, pi)
        cmul_conj(vrow[4], vrow[5], qr, qi, L["Brow"], L["Birow"])
        cmul_conj(vrow[6], vrow[7], qr, qi, pr, pi)
        mk.inc(vector.engine_nop(), "tv", 1)
    t_v = mk.tok("tv")

    @block.tensor
    def _(tensor):
        mk.wait(tensor, "tv", t_v)
        mk.wait(tensor, "tsc", t_s2)
        lam_re = L["lamr"][0:1, 0:128:2]; lam_im = L["lamr"][0:1, 1:128:2]
        tensor.matmul(e["p512"][3][:, 0:64], onesr, lam_re, start=True, stop=True)
        tensor.matmul(e["p512"][3][:, 64:128], onesr, lam_im, start=True, stop=True)
        mk.inc(tensor.matmul(e["p512"][3][:, 128:129], onesr, L["s2row"], start=True, stop=True), "tt", 1)
        for i in range(8):
            ins = tensor.matmul(e["p512"][2][:, 64 * i:64 * i + 64], onesr, L["vrow"][i], start=True, stop=True)
        mk.inc(ins, "tt", 1)
    t_bc = mk.tok("tt")

    @block.vector
    def _(vector):
        mk.wait(vector, "tt", t_bc)
        vector.tensor_scalar(out=L["nLr"], in0=e["p512"][3][:, 0:64], scalar1=-1.0, scalar2=None, op0=ALU.mult)
        vector.tensor_scalar(out=L["nLi"], in0=e["p512"][3][:, 64:128], scalar1=-1.0, scalar2=None, op0=ALU.mult)
        vector.tensor_copy(out=L["s2b"], in_=e["p512"][3][:, 128:129])
        for i in range(8):
            vector.tensor_copy(out=L["vB"][i], in_=e["p512"][2][:, 64 * i:64 * i + 64])
        mk.guard(vector)
        vector.tensor_scalar(out=L["gr"], in0=L["t1r"], scalar1=L["s2b"], scalar2=None, op0=ALU.mult)
        vector.tensor_scalar(out=L["gi"], in0=L["t1i"], scalar1=L["s2b"], scalar2=None, op0=ALU.mult)
        mk.inc(vector.engine_nop(), "tv", 1)
    return mk.tok("tv")


def phase_k_b(mk, block, e, L):
    @block.vector
    def _(vector):
        KR3 = L["KR3"]
        dra = KR3(2048); dia = KR3(4096); t1a = KR3(6144)
        t2a = KR3(8192); Pa = KR3(10240); Qa = KR3(12288)
        grX = L["gr"][:, :, None].broadcast_to([128, 32, 64])
        giX = L["gi"][:, :, None].broadcast_to([128, 32, 64])
        nLrX = L["nLr"][:, None, :].broadcast_to([128, 32, 64])
        nLiX = L["nLi"][:, None, :].broadcast_to([128, 32, 64])
        vector.tensor_tensor(out=dra, in0=grX, in1=nLrX, op=ALU.add)
        vector.tensor_tensor(out=dia, in0=giX, in1=nLiX, op=ALU.add)
        vector.tensor_tensor(out=t1a, in0=dra, in1=dra, op=ALU.mult)
        vector.tensor_tensor(out=t2a, in0=dia, in1=dia, op=ALU.mult)
        vector.tensor_tensor(out=t1a, in0=t1a, in1=t2a, op=ALU.add)
        vector.reciprocal(out=t2a, in_=t1a)
        vector.tensor_tensor(out=Pa, in0=dra, in1=t2a, op=ALU.mult)
        vector.tensor_tensor(out=Qa, in0=dia, in1=t2a, op=ALU.mult)
        kx = L["kx"]
        for vi_ in range(4):
            vrX = L["vB"][2 * vi_][:, None, :].broadcast_to([128, 32, 64])
            viX = L["vB"][2 * vi_ + 1][:, None, :].broadcast_to([128, 32, 64])
            vector.tensor_tensor(out=t1a, in0=Pa, in1=vrX, op=ALU.mult)
            vector.tensor_tensor(out=t2a, in0=Qa, in1=viX, op=ALU.mult)
            vector.tensor_tensor(out=t1a, in0=t1a, in1=t2a, op=ALU.add)
            vector.reduce_sum(out=kx[2 * vi_], in_=t1a, axis=AX.X)
            vector.tensor_tensor(out=t1a, in0=Pa, in1=viX, op=ALU.mult)
            vector.tensor_tensor(out=t2a, in0=Qa, in1=vrX, op=ALU.mult)
            vector.tensor_tensor(out=t1a, in0=t1a, in1=t2a, op=ALU.subtract)
            vector.reduce_sum(out=kx[2 * vi_ + 1], in_=t1a, axis=AX.X)
        mk.guard(vector)
        k00r, k00i, k01r, k01i, k10r, k10i, k11r, k11i = kx
        G = lambda: mk.guard(vector)
        t1, t2 = L["t1"], L["t2"]
        tmp, tmp2, tmp3 = L["tmp"], L["tmp2"], L["tmp3"]
        arre, aim = L["arre"], L["aim"]
        arre2, aim2 = L["arre2"], L["aim2"]
        denr = t1[:, 0:32]
        vector.tensor_scalar(out=denr, in0=k11r, scalar1=1.0, scalar2=None, op0=ALU.add)
        G()
        vector.tensor_tensor(out=tmp, in0=denr, in1=denr, op=ALU.mult)
        vector.tensor_tensor(out=tmp2, in0=k11i, in1=k11i, op=ALU.mult)
        G()
        vector.tensor_tensor(out=tmp, in0=tmp, in1=tmp2, op=ALU.add)
        G()
        vector.reciprocal(out=tmp3, in_=tmp)
        G()
        zr = t2[:, 0:32]; zi = t2[:, 32:64]
        vector.tensor_tensor(out=zr, in0=k01r, in1=k10r, op=ALU.mult)
        vector.tensor_tensor(out=tmp, in0=k01i, in1=k10i, op=ALU.mult)
        vector.tensor_tensor(out=zi, in0=k01r, in1=k10i, op=ALU.mult)
        vector.tensor_tensor(out=tmp2, in0=k01i, in1=k10r, op=ALU.mult)
        G()
        vector.tensor_tensor(out=zr, in0=zr, in1=tmp, op=ALU.subtract)
        vector.tensor_tensor(out=zi, in0=zi, in1=tmp2, op=ALU.add)
        G()
        vector.tensor_tensor(out=tmp, in0=zr, in1=denr, op=ALU.mult)
        vector.tensor_tensor(out=tmp2, in0=zi, in1=k11i, op=ALU.mult)
        G()
        vector.tensor_tensor(out=tmp, in0=tmp, in1=tmp2, op=ALU.add)
        G()
        vector.tensor_tensor(out=arre, in0=tmp, in1=tmp3, op=ALU.mult)
        vector.tensor_tensor(out=tmp, in0=zi, in1=denr, op=ALU.mult)
        vector.tensor_tensor(out=tmp2, in0=zr, in1=k11i, op=ALU.mult)
        G()
        vector.tensor_tensor(out=tmp, in0=tmp, in1=tmp2, op=ALU.subtract)
        G()
        vector.tensor_tensor(out=aim, in0=tmp, in1=tmp3, op=ALU.mult)
        G()
        vector.tensor_tensor(out=arre2, in0=k00r, in1=arre, op=ALU.subtract)
        vector.tensor_tensor(out=aim2, in0=k00i, in1=aim, op=ALU.subtract)
        G()
        vector.tensor_tensor(out=tmp, in0=L["ccr"], in1=arre2, op=ALU.mult)
        vector.tensor_tensor(out=tmp2, in0=L["cci"], in1=aim2, op=ALU.mult)
        vector.tensor_tensor(out=tmp3, in0=L["ccr"], in1=aim2, op=ALU.mult)
        vector.tensor_tensor(out=t1[:, 32:64], in0=L["cci"], in1=arre2, op=ALU.mult)
        G()
        vector.tensor_tensor(out=arre, in0=tmp, in1=tmp2, op=ALU.subtract)
        vector.tensor_tensor(out=tmp, in0=tmp3, in1=t1[:, 32:64], op=ALU.add)
        G()
        vector.tensor_scalar(out=aim, in0=tmp, scalar1=-1.0, scalar2=None, op0=ALU.mult)
        mk.inc(vector.engine_nop(), "tv", 1)
    return mk.tok("tv")


def phase_k_c(mk, block, e, L, t_at):
    nc = mk.nc
    p512 = e["p512"]
    ident32 = e["ident32"]
    ktile = e["ktile"]
    arre, aim = L["arre"], L["aim"]

    @block.sync
    def _(sync):
        mk.wait(sync, "tv", mk.tok("tv"))
        for dst, srcc in [(L["twc"], e["twc"]), (L["tws"], e["tws"]),
                          (L["f64c2"], e["f64c2"]), (L["f64s2n"], e["f64s2n"])]:
            mk.inc(sync.dma_start(out=dst, in_=srcc), "sd0", 16)
    kc_ld = mk.tok("sd0")

    @block.tensor
    def _(tensor):
        mk.wait(tensor, "tv", mk.tok("tv"))
        for h in range(2):
            pp = slice(64 * h, 64 * h + 64)
            sl = slice(32 * h, 32 * h + 32)
            tensor.matmul(p512[0][0:64, sl], L["f64c"][pp, :], arre[pp, :], start=True, stop=False)
            mk.inc(tensor.matmul(p512[0][0:64, sl], L["f64sn"][pp, :], aim[pp, :], start=False, stop=True, skip_group_check=True), "tt", 1)
            tensor.matmul(p512[1][0:64, sl], L["f64s"][pp, :], arre[pp, :], start=True, stop=False)
            mk.inc(tensor.matmul(p512[1][0:64, sl], L["f64c"][pp, :], aim[pp, :], start=False, stop=True, skip_group_check=True), "tt", 1)
    t_mm1 = mk.tok("tt")

    @block.vector
    def _(vector):
        mk.wait(vector, "tt", t_mm1)
        mk.wait(vector, "sd0", kc_ld)
        vector.tensor_copy(out=L["srT"], in_=p512[0][0:64, 0:64])
        vector.tensor_copy(out=L["siT"], in_=p512[1][0:64, 0:64])
        mk.guard(vector)
        vector.tensor_tensor(out=L["spr"], in0=L["srT"], in1=L["twc"], op=ALU.mult)
        vector.tensor_tensor(out=L["spi"], in0=L["siT"], in1=L["tws"], op=ALU.mult)
        vector.tensor_tensor(out=L["sprT"], in0=L["srT"], in1=L["tws"], op=ALU.mult)
        vector.tensor_tensor(out=L["spiT"], in0=L["siT"], in1=L["twc"], op=ALU.mult)
        mk.guard(vector)
        vector.tensor_tensor(out=L["tw1"], in0=L["spr"], in1=L["spi"], op=ALU.subtract)
        vector.tensor_tensor(out=L["tw2"], in0=L["sprT"], in1=L["spiT"], op=ALU.add)
        mk.inc(vector.engine_nop(), "tv", 1)
    t_tw = mk.tok("tv")

    @block.tensor
    def _(tensor):
        mk.wait(tensor, "tv", t_tw)
        mk.inc(tensor.transpose(p512[2][0:64, 0:64], L["tw1"], ident32[0:64, 0:64]), "tt", 1)
        mk.inc(tensor.transpose(p512[3][0:64, 0:64], L["tw2"], ident32[0:64, 0:64]), "tt", 1)
    t_spt = mk.tok("tt")

    @block.vector
    def _(vector):
        mk.wait(vector, "tt", t_spt)
        vector.tensor_copy(out=L["sprT"], in_=p512[2][0:64, 0:64])
        vector.tensor_copy(out=L["spiT"], in_=p512[3][0:64, 0:64])
        mk.inc(vector.engine_nop(), "tv", 1)
    t_spe = mk.tok("tv")

    @block.tensor
    def _(tensor):
        mk.wait(tensor, "tv", t_spe)
        tensor.matmul(p512[0][0:64, 0:64], L["f64c2"], L["sprT"], start=True, stop=False)
        mk.inc(tensor.matmul(p512[0][0:64, 0:64], L["f64s2n"], L["spiT"], start=False, stop=True, skip_group_check=True), "tt", 1)
    t_mm2 = mk.tok("tt")

    @block.vector
    def _(vector):
        mk.wait(vector, "tt", t_mm2)
        vector.tensor_copy(out=ktile[:], in_=p512[0][0:64, 0:64])
        vector.memset(e["av"](e["AG"], 0, 4096, BF16), 0.0)
        mk.inc(vector.engine_nop(), "tv", 1)
    t_kt = mk.tok("tv")

    @block.sync
    def _(sync):
        mk.wait(sync, "tv", t_kt)
        mk.inc(sync.dma_start(out=e["buf3"][0, 0:4096], in_=e["av"](e["AG"], 0, 4096, BF16, 1, 0)), "sdo", 16)
        dstap = bass.AP(e["buf3"].tensor, 4096, [[64, 64], [1, 64]])
        mk.inc(sync.dma_start(out=dstap, in_=ktile[:]), "sdo", 16)
    return mk.tok("sdo")


ALL_SEMS = ["sd0", "sd1", "sw0", "sw1", "sq", "sgq", "tv", "tsc", "tt", "sdo"]


def barrier(mk, block, engines=("sync", "vector", "scalar", "tensor", "gpsimd")):
    toks = {s: mk.tok(s) for s in ALL_SEMS}
    for en in engines:
        def _(eng, toks=toks):
            for s, v in toks.items():
                if v > 0:
                    eng.wait_ge(mk.sem[s], v)
        getattr(block, en)(_)


def ln_pass(mk, block, e, src, rev=False, dscale=False, stats_in=False):
    nc = mk.nc
    av, AG = e["av"], e["AG"]
    stats = e["stats"]
    arena = e["arena"]
    xsl = [av(AG, 0, 4096, F32), av(AG, 8192, 4096, F32)]
    x16l = [av(AG, 16384, 4096, BF16), av(AG, 20480, 4096, BF16)]
    junkA = av(AG, 24576, 4096, BF16)
    junkB = av(AG, 28672, 4096, BF16)

    if dscale:
        ddrow = av(e["AT"], 0, 1, F32, 1, 0)

        def _(sync):
            mk.inc(sync.dma_start(out=ddrow, in_=e["dd_in"]), "sd0", 16)
        block.sync(_)
        ddtok = mk.tok("sd0")

        def _(tensor):
            mk.wait(tensor, "sd0", ddtok)
            mk.wait(tensor, "tv", mk.tok("tv"))
            mk.inc(tensor.matmul(e["p512"][3][:, 0:1], e["onesr"], ddrow, start=True, stop=True), "tt", 1)
        block.tensor(_)
        ddmm = mk.tok("tt")

        def _(vector):
            mk.wait(vector, "tt", ddmm)
            vector.tensor_copy(out=stats[:, 57:58], in_=e["p512"][3][:, 0:1])
            mk.inc(vector.engine_nop(), "tv", 1)
        block.vector(_)

    if stats_in:
        def _(vector):
            G = lambda: mk.guard(vector)
            A = stats[:, 0:8]; B = stats[:, 8:16]
            M = stats[:, 16:24]; R = stats[:, 24:32]; N = stats[:, 32:40]
            D1 = stats[:, 40:48]; D2 = stats[:, 48:56]
            S = stats[:, 64:72]; S2 = stats[:, 72:80]; S3 = stats[:, 56:57]
            for r8 in range(8):
                vector.reduce_sum(out=A[:, r8:r8 + 1], in_=e["pS"][:, 8 * r8:8 * r8 + 8], axis=AX.X)
                vector.reduce_sum(out=B[:, r8:r8 + 1], in_=e["pQ"][:, 8 * r8:8 * r8 + 8], axis=AX.X)
            G()
            vector.tensor_scalar(out=M, in0=A, scalar1=1.0 / 4096.0, scalar2=None, op0=ALU.mult)
            vector.tensor_scalar(out=S, in0=B, scalar1=1.0 / 4096.0, scalar2=None, op0=ALU.mult)
            G()
            vector.tensor_tensor(out=A, in0=M, in1=M, op=ALU.mult)
            G()
            vector.tensor_tensor(out=B, in0=S, in1=A, op=ALU.subtract)
            G()
            vector.tensor_scalar(out=S, in0=B, scalar1=EPS, scalar2=None, op0=ALU.add)
            mk.inc(vector.engine_nop(), "tv", 1)
        block.vector(_)
        ve_tok = mk.tok("tv")

        def _(scalar):
            mk.wait(scalar, "tv", ve_tok)
            mk.inc(scalar.activation(out=stats[:, 0:8], in_=stats[:, 64:72], func=AF.Sqrt), "tsc", 1)
        block.scalar(_)
        sq_tok = mk.tok("tsc")

        def _(vector):
            G = lambda: mk.guard(vector)
            mk.wait(vector, "tsc", sq_tok)
            A = stats[:, 0:8]; B = stats[:, 8:16]
            M = stats[:, 16:24]; R = stats[:, 24:32]; N = stats[:, 32:40]
            D1 = stats[:, 40:48]; D2 = stats[:, 48:56]
            S = stats[:, 64:72]; S2 = stats[:, 72:80]
            vector.reciprocal(out=B, in_=A)
            G()
            vector.tensor_tensor(out=A, in0=B, in1=B, op=ALU.mult)
            G()
            vector.tensor_tensor(out=A, in0=A, in1=S, op=ALU.mult)
            G()
            vector.tensor_scalar(out=A, in0=A, scalar1=-0.5, scalar2=1.5, op0=ALU.mult, op1=ALU.add)
            G()
            vector.tensor_tensor(out=S2, in0=B, in1=A, op=ALU.mult)
            G()
            vector.tensor_tensor(out=A, in0=S2, in1=S2, op=ALU.mult)
            G()
            vector.tensor_tensor(out=A, in0=A, in1=S, op=ALU.mult)
            G()
            vector.tensor_scalar(out=A, in0=A, scalar1=-0.5, scalar2=1.5, op0=ALU.mult, op1=ALU.add)
            G()
            vector.tensor_tensor(out=R, in0=S2, in1=A, op=ALU.mult)
            G()
            vector.tensor_scalar(out=A, in0=M, scalar1=-1.0, scalar2=None, op0=ALU.mult)
            G()
            vector.tensor_tensor(out=N, in0=A, in1=R, op=ALU.mult)
            G()
            if dscale:
                ddx = stats[:, 57:58].broadcast_to([128, 8])
                vector.tensor_tensor(out=S2, in0=R, in1=ddx, op=ALU.mult)
                vector.tensor_tensor(out=D2, in0=N, in1=ddx, op=ALU.mult)
                G()
                vector.tensor_scalar(out=D1, in0=S2, scalar1=1.0, scalar2=None, op0=ALU.add)
                G()
            mk.inc(vector.engine_nop(), "tv", 1)
        block.vector(_)

        last_v_use = [0, 0]
        st_used = [0, 0]
        for r8 in range(8):
            p = r8 & 1
            sem = "sd0" if p == 0 else "sd1"

            def _(sync, r8=r8, p=p, sem=sem):
                if last_v_use[p] > 0:
                    mk.wait(sync, "tv", last_v_use[p])
                mk.inc(sync.dma_start(out=xsl[p], in_=src[r8 * 128:(r8 + 1) * 128, :]), sem, 16)
            block.sync(_)
            ld_tok = mk.tok(sem)

            def _(vector, r8=r8, p=p, sem=sem, ld_tok=ld_tok):
                mk.wait(vector, sem, ld_tok)
                if st_used[p] > 0:
                    mk.wait(vector, "sdo", st_used[p])
                rs = stats[:, 24 + r8:25 + r8]
                nmr = stats[:, 32 + r8:33 + r8]
                if rev:
                    h = x16l[p]
                    outap = bass.AP(h.tensor, h.offset + 4095, [list(h.ap[0]), [-1, 4096]])
                else:
                    outap = x16l[p]
                vector.tensor_scalar(out=outap, in0=xsl[p], scalar1=rs, scalar2=nmr,
                                     op0=ALU.mult, op1=ALU.add)
                mk.inc(vector.engine_nop(), "tv", 1)
                last_v_use[p] = mk.tok("tv")
            block.vector(_)
            x16_tok = mk.tok("tv")

            def _(sync, r8=r8, p=p, x16_tok=x16_tok):
                mk.wait(sync, "tv", x16_tok)
                mk.inc(sync.dma_start(out=e["bounce"][r8 * 128:(r8 + 1) * 128, :], in_=x16l[p]), "sdo", 16)
            block.sync(_)
            st_used[p] = mk.tok("sdo")
        return mk.tok("sdo")

    def _(vector):
        vector.memset(stats[:, 0:16], 0.0)
        mk.inc(vector.engine_nop(), "tv", 1)
    block.vector(_)
    z_tok = mk.tok("tv")

    def _(scalar):
        mk.wait(scalar, "tv", z_tok)
        scalar.copy(out=stats[:, 0:1], in_=stats[:, 0:1])
    block.scalar(_)

    last_v_use = [0, 0]
    st_used = [0, 0]
    for r8 in range(8):
        p = r8 & 1
        sem = "sd0" if p == 0 else "sd1"

        def _(sync, r8=r8, p=p, sem=sem):
            if last_v_use[p] > 0:
                mk.wait(sync, "tv", last_v_use[p])
            mk.inc(sync.dma_start(out=xsl[p], in_=src[r8 * 128:(r8 + 1) * 128, :]), sem, 16)
        block.sync(_)
        ld_tok = mk.tok(sem)

        def _(vector, r8=r8, p=p, sem=sem, ld_tok=ld_tok):
            mk.wait(vector, sem, ld_tok)
            vector.tensor_scalar(out=junkA, in0=xsl[p], scalar1=1.0, scalar2=None,
                                 op0=ALU.mult, op1=ALU.add, accum_out=stats[:, r8:r8 + 1])
            mk.inc(vector.engine_nop(), "tv", 1)
        block.vector(_)
        vsum_tok = mk.tok("tv")

        def _(scalar, r8=r8, p=p, sem=sem, ld_tok=ld_tok):
            mk.wait(scalar, sem, ld_tok)
            mk.inc(scalar.activation(out=junkB, in_=xsl[p], func=AF.Square,
                                     accum_out=stats[:, 8 + r8:9 + r8]), "tsc", 1)
        block.scalar(_)
        ssq_tok = mk.tok("tsc")

        def _(vector, r8=r8, ssq_tok=ssq_tok):
            mk.wait(vector, "tsc", ssq_tok)
            m = stats[:, 16 + r8:17 + r8]
            t0 = stats[:, 58:59]; t1 = stats[:, 59:60]; t2 = stats[:, 60:61]
            vector.tensor_scalar(out=m, in0=stats[:, r8:r8 + 1], scalar1=1.0 / 4096.0,
                                 scalar2=None, op0=ALU.mult)
            vector.tensor_scalar(out=t1, in0=stats[:, 8 + r8:9 + r8], scalar1=1.0 / 4096.0,
                                 scalar2=None, op0=ALU.mult)
            mk.guard(vector)
            vector.tensor_tensor(out=t0, in0=m, in1=m, op=ALU.mult)
            mk.guard(vector)
            vector.tensor_tensor(out=t1, in0=t1, in1=t0, op=ALU.subtract)
            mk.guard(vector)
            vector.tensor_scalar(out=t1, in0=t1, scalar1=EPS, scalar2=None, op0=ALU.add)
            mk.inc(vector.engine_nop(), "tv", 1)
        block.vector(_)
        veps_tok = mk.tok("tv")

        def _(scalar, r8=r8, veps_tok=veps_tok):
            mk.wait(scalar, "tv", veps_tok)
            mk.inc(scalar.activation(out=stats[:, 61:62], in_=stats[:, 59:60], func=AF.Sqrt), "tsc", 1)
        block.scalar(_)
        sq_tok = mk.tok("tsc")

        def _(vector, r8=r8, p=p, sq_tok=sq_tok):
            mk.wait(vector, "tsc", sq_tok)
            if st_used[p] > 0:
                mk.wait(vector, "sdo", st_used[p])
            m = stats[:, 16 + r8:17 + r8]
            rs = stats[:, 24 + r8:25 + r8]
            nmr = stats[:, 32 + r8:33 + r8]
            t1 = stats[:, 59:60]
            t2 = stats[:, 60:61]
            t3 = stats[:, 62:63]
            vector.reciprocal(out=t2, in_=stats[:, 61:62])
            mk.guard(vector)
            vector.tensor_tensor(out=t3, in0=t2, in1=t2, op=ALU.mult)
            mk.guard(vector)
            vector.tensor_tensor(out=t3, in0=t3, in1=t1, op=ALU.mult)
            mk.guard(vector)
            vector.tensor_scalar(out=t3, in0=t3, scalar1=-0.5, scalar2=1.5, op0=ALU.mult, op1=ALU.add)
            mk.guard(vector)
            vector.tensor_tensor(out=stats[:, 63:64], in0=t2, in1=t3, op=ALU.mult)
            mk.guard(vector)
            vector.tensor_tensor(out=t3, in0=stats[:, 63:64], in1=stats[:, 63:64], op=ALU.mult)
            mk.guard(vector)
            vector.tensor_tensor(out=t3, in0=t3, in1=t1, op=ALU.mult)
            mk.guard(vector)
            vector.tensor_scalar(out=t3, in0=t3, scalar1=-0.5, scalar2=1.5, op0=ALU.mult, op1=ALU.add)
            mk.guard(vector)
            vector.tensor_tensor(out=rs, in0=stats[:, 63:64], in1=t3, op=ALU.mult)
            vector.tensor_scalar(out=t3, in0=m, scalar1=-1.0, scalar2=None, op0=ALU.mult)
            mk.guard(vector)
            vector.tensor_tensor(out=nmr, in0=t3, in1=rs, op=ALU.mult)
            mk.guard(vector)
            if dscale:
                dd1 = stats[:, 57:58]
                vector.tensor_tensor(out=stats[:, 40 + r8:41 + r8], in0=rs, in1=dd1, op=ALU.mult)
                vector.tensor_tensor(out=stats[:, 48 + r8:49 + r8], in0=nmr, in1=dd1, op=ALU.mult)
                mk.guard(vector)
                vector.tensor_scalar(out=stats[:, 40 + r8:41 + r8], in0=stats[:, 40 + r8:41 + r8],
                                     scalar1=1.0, scalar2=None, op0=ALU.add)
                mk.guard(vector)
            if rev:
                h = x16l[p]
                outap = bass.AP(h.tensor, h.offset + 4095, [list(h.ap[0]), [-1, 4096]])
            else:
                outap = x16l[p]
            vector.tensor_scalar(out=outap, in0=xsl[p], scalar1=rs, scalar2=nmr,
                                 op0=ALU.mult, op1=ALU.add)
            mk.inc(vector.engine_nop(), "tv", 1)
            last_v_use[p] = mk.tok("tv")
        block.vector(_)
        x16_tok = mk.tok("tv")

        def _(sync, r8=r8, p=p, x16_tok=x16_tok):
            mk.wait(sync, "tv", x16_tok)
            mk.inc(sync.dma_start(out=e["bounce"][r8 * 128:(r8 + 1) * 128, :], in_=x16l[p]), "sdo", 16)
        block.sync(_)
        st_used[p] = mk.tok("sdo")
    return mk.tok("sdo")


def tload(mk, block, e, dst_base, store_tok):
    av = e["av"]

    def _(scalar):
        mk.wait(scalar, "sdo", store_tok)
        for c in range(32):
            mk.inc(scalar.dma_start(
                out=av(dst_base, c * 1024, 1024, BF16),
                in_=e["bounce"][:, c * 128:(c + 1) * 128], transpose=True), "sq", 16)
    block.scalar(_)
    return mk.tok("sq")


def build_q_units(mk, block, e, in_chunks, dst_chunks, inv_tile, nch, half_scale,
                  xs_tmp, row_off, deps=()):
    nc = mk.nc
    qsc, qtmp = e["qsc"], e["qtmp"]
    mxt32 = qtmp[:, 0:2048].bitcast(F32)
    mnt32 = qtmp[:, 2048:4096].bitcast(F32)
    slot3 = qtmp[:, 4096:6144].bitcast(F32)
    tmp32 = qtmp[:, 0:2048].bitcast(I32)
    invscB32 = qtmp[:, 2048:4096].bitcast(F32)
    p512, pbig0, pbig1 = e["p512"], e["pbig0"], e["pbig1"]
    units = []
    state = {}

    for u0 in range(0, nch, 2):
        def _u(u0=u0):
            def _(vector):
                if u0 == 0:
                    for s, t in deps:
                        mk.wait(vector, s, t)
                for c in range(u0, min(u0 + 2, nch)):
                    vector.tensor_scalar(out=xs_tmp, in0=in_chunks[c],
                                         scalar1=inv_tile[:, c:c + 1], scalar2=None, op0=ALU.mult)
                    if c == 0:
                        vector.tensor_copy(out=mxt32, in_=xs_tmp)
                        vector.tensor_copy(out=mnt32, in_=xs_tmp)
                    else:
                        vector.tensor_tensor(out=mxt32, in0=mxt32, in1=xs_tmp, op=ALU.max)
                        vector.tensor_tensor(out=mnt32, in0=mnt32, in1=xs_tmp, op=ALU.min)
                mk.inc(vector.engine_nop(), "tv", 1)
            block.vector(_)
        units.append(_u)

    def _u_neg():
        def _(vector):
            mk.guard(vector)
            vector.tensor_scalar(out=slot3, in0=mnt32, scalar1=-1.0, scalar2=None, op0=ALU.mult)
            mk.inc(vector.engine_nop(), "tv", 1)
        block.vector(_)
        state["tree_tok"] = mk.tok("tv")
    units.append(_u_neg)

    def bank_mx(k):
        return p512[k // 4][:, 128 * (k % 4):128 * (k % 4) + 128]

    def bank_mn(k):
        return p512[2 + k // 4][:, 128 * (k % 4):128 * (k % 4) + 128]

    def _u_tr():
        tvfull = mk.tok("tv")
        def _(tensor):
            mk.wait(tensor, "tv", max(state["tree_tok"], tvfull))
            for s, t in deps:
                mk.wait(tensor, s, t)
            for k in range(8):
                tensor.transpose(bank_mx(k), mxt32[:, 128 * k:128 * k + 128], e["ident32"])
                ins = tensor.transpose(bank_mn(k), slot3[:, 128 * k:128 * k + 128], e["ident32"])
            mk.inc(ins, "tt", 1)
        block.tensor(_)
        state["tr_tok"] = mk.tok("tt")
    units.append(_u_tr)

    def _u_sc():
        def _(vector):
            mk.wait(vector, "tt", state["tr_tok"])
            for k in range(8):
                vector.reduce_max(out=qsc[:, k:k + 1], in_=bank_mx(k), axis=AX.X)
                vector.reduce_max(out=qsc[:, 32 + k:33 + k], in_=bank_mn(k), axis=AX.X)
            mk.guard(vector)
            vector.tensor_tensor(out=qsc[:, 40:48], in0=qsc[:, 0:8], in1=qsc[:, 32:40], op=ALU.max)
            mk.guard(vector)
            vector.tensor_scalar(out=qsc[:, 8:16], in0=qsc[:, 40:48], scalar1=1.0 / 127.0,
                                 scalar2=1e-8, op0=ALU.mult, op1=ALU.max)
            mk.guard(vector)
            vector.reciprocal(out=qsc[:, 16:24], in_=qsc[:, 8:16])
            mk.guard(vector)
            if half_scale:
                vector.tensor_scalar(out=qsc[:, 24:32], in0=qsc[:, 8:16], scalar1=0.5, scalar2=None, op0=ALU.mult)
                mk.guard(vector)
            mk.inc(vector.engine_nop(), "tv", 1)
        block.vector(_)
        state["sc_tok"] = mk.tok("tv")
    units.append(_u_sc)

    def _u_qt():
        def _(tensor):
            mk.wait(tensor, "tv", state["sc_tok"])
            mk.inc(tensor.transpose(p512[2][0:16, 0:128], qsc[:, 8:24], e["ident32"]), "tt", 1)
        block.tensor(_)
        state["qt_tok"] = mk.tok("tt")
    units.append(_u_qt)

    def _u_qr():
        def _(vector):
            mk.wait(vector, "tt", state["qt_tok"])
            mk.wait(vector, "sdo", mk.tok("sdo"))
            vector.tensor_copy(out=e["qrow"][:], in_=p512[2][0:16, 0:128])
            mk.inc(vector.engine_nop(), "tv", 1)
        block.vector(_)
        state["qr_tok"] = mk.tok("tv")
    units.append(_u_qr)

    def _u_st():
        def _(sync):
            mk.wait(sync, "tv", state["qr_tok"])
            mk.inc(sync.dma_start(out=e["row_sc"][0, row_off:row_off + 1024], in_=e["qrow"][0:8, :]), "sdo", 16)
            mk.inc(sync.dma_start(out=e["row_sc"][0, row_off + 1024:row_off + 2048], in_=e["qrow"][8:16, :]), "sdo", 16)
        block.sync(_)
        state["st_tok"] = mk.tok("sdo")

        def _(gpsimd):
            mk.wait(gpsimd, "sdo", state["st_tok"])
            rep = bass.AP(e["row_sc"].tensor, row_off + 1024, [[0, 128], [1, 1024]])
            mk.inc(gpsimd.dma_start(out=invscB32[:, 0:1024], in_=rep), "sgq", 16)
        block.gpsimd(_)
        state["ivb"] = mk.tok("sgq")
    units.append(_u_st)

    for u0 in range(0, nch, 2):
        def _u(u0=u0):
            def _(vector):
                if u0 == 0:
                    mk.wait(vector, "sgq", state["ivb"])
                    mk.guard(vector)
                for c in range(u0, min(u0 + 2, nch)):
                    vector.scalar_tensor_tensor(out=tmp32[:, 0:1024], in0=in_chunks[c],
                                                scalar=inv_tile[:, c:c + 1], in1=invscB32[:, 0:1024],
                                                op0=ALU.mult, op1=ALU.mult)
                    vector.tensor_copy(out=dst_chunks[c], in_=tmp32[:, 0:1024])
                mk.inc(vector.engine_nop(), "tv", 1)
            block.vector(_)
        units.append(_u)
    return units


def emit_units(units):
    for u in units:
        u()


def emit_scB_load(mk, block, e, row_off):
    def _(sync):
        mk.wait(sync, "tv", mk.tok("tv"))
        rep = bass.AP(e["row_sc"].tensor, row_off, [[0, 128], [1, 1024]])
        mk.inc(sync.dma_start(out=e["scB"][:], in_=rep), "sd0", 16)
    block.sync(_)
    return mk.tok("sd0")


def phase_ffn(mk, block, e, src, dst, hook_after_gate=None, hook_after_up=None,
              stats_in=False, collect_stats=False):
    nc = mk.nc
    av, AXr, AG, AT = e["av"], e["AX"], e["AG"], e["AT"]
    p512, pbig0, pbig1 = e["p512"], e["pbig0"], e["pbig1"]
    wslab, qsc = e["wslab"], e["qsc"]

    barrier(mk, block)
    st_tok = ln_pass(mk, block, e, src, rev=False, dscale=False, stats_in=stats_in)
    if e.get("dbg"):
        def _(sync):
            mk.wait(sync, "tv", mk.tok("tv"))
            mk.inc(sync.dma_start(out=e["dbg_st"], in_=e["stats"][:, 0:64]), "sdo", 16)
        block.sync(_)
        st_tok = mk.tok("sdo")
    tl_tok = tload(mk, block, e, AXr, st_tok)

    bT = [av(AXr, c * 1024, 1024, BF16) for c in range(32)]
    bq = [av(AG, c * 1024, 1024, BF16) for c in range(32)]

    def mm_block(wmat, s_w, qtok, is_gate, filler=None, qbuf=None):
        qbuf = bq if qbuf is None else qbuf
        slab_consumed = e.setdefault("wslab_consumed", [0, 0])
        filler_popped = [0]
        filler_n_p1 = 8 if (filler and not is_gate) else 0
        """16 blks x (4 quarters x 8 j2 x 2 rc) matmuls; evac per blk."""
        slab_tok = [0, 0]
        evac_tok = [0, 0]
        sgc = [av(AT, 16384 + 2048, 1024, BF16), av(AT, 16384 + 3072, 1024, BF16)]
        gA = av(AT, 16384 + 4096, 1024, F32)
        sgl_tok = [0, 0]
        for blk in range(16):
            psum = pbig0 if (blk & 1) == 0 else pbig1
            for hj in range(2):
                idx = 2 * blk + hj
                slot = idx & 1
                sem = "sw0" if slot == 0 else "sw1"

                def _(gpsimd, blk=blk, hj=hj, idx=idx, slot=slot, sem=sem):
                    if slab_consumed[slot] > 0:
                        mk.wait(gpsimd, "tt", slab_consumed[slot])
                    mk.inc(gpsimd.dma_start(
                        out=wslab[:, slot, :],
                        in_=wmat[blk, :, 16 * hj:16 * hj + 16, :].rearrange("p a b -> p (a b)")), sem, 16)
                block.gpsimd(_)
                slab_tok[slot] = mk.tok(sem)

                def _(tensor, blk=blk, hj=hj, idx=idx, slot=slot, sem=sem,
                      psum=psum, stok=slab_tok[slot]):
                    mk.wait(tensor, sem, stok)
                    if hj == 0:
                        mk.wait(tensor, "tv", qtok)
                        if evac_tok[blk & 1] > 0:
                            mk.wait(tensor, "tv", evac_tok[blk & 1])
                    for j2 in range(16):
                        j = 16 * hj + j2
                        for rc in range(2):
                            ins = tensor.matmul(
                                psum[:, rc * 512:rc * 512 + 512],
                                wslab[:, slot, j2 * 128:j2 * 128 + 128],
                                qbuf[j][:, rc * 512:rc * 512 + 512],
                                start=(j == 0), stop=(j == 31),
                                skip_group_check=True)
                    mk.inc(ins, "tt", 1)
                block.tensor(_)
                slab_consumed[slot] = mk.tok("tt")
            mm_done = mk.tok("tt")

            if is_gate:
                def _(vector, blk=blk, psum=psum, mm_done=mm_done, sctok=mk.tok("sd0")):
                    mk.wait(vector, "tt", mm_done)
                    mk.wait(vector, "sd0", sctok)
                    vector.scalar_tensor_tensor(out=gA, in0=psum[:], scalar=e["s_swg"][:, blk:blk + 1],
                                                in1=e["scB"][:], op0=ALU.mult, op1=ALU.mult)
                    mk.inc(vector.engine_nop(), "tv", 1)
                block.vector(_)
                evac_tok[blk & 1] = mk.tok("tv")
                ga_tok = mk.tok("tv")
                if filler:
                    want = max(1, (len(filler) + 15 - blk) // (16 - blk))
                    while want > 0 and filler:
                        if filler_popped[0] < filler_n_p1 and filler_popped[0] >= (blk + 1) // 2:
                            break
                        filler.popleft()()
                        filler_popped[0] += 1
                        want -= 1

                def _(scalar, blk=blk, ga_tok=ga_tok):
                    mk.wait(scalar, "tv", ga_tok)
                    if sgl_tok[blk & 1] > 0:
                        mk.wait(scalar, "sdo", sgl_tok[blk & 1])
                    mk.inc(scalar.activation(out=sgc[blk & 1], in_=gA, func=AF.Silu), "tsc", 1)
                block.scalar(_)
                si_tok = mk.tok("tsc")

                def _(sync, blk=blk, si_tok=si_tok):
                    mk.wait(sync, "tsc", si_tok)
                    mk.inc(sync.dma_start(out=e["sg_park"][blk * 128:(blk + 1) * 128, :],
                                          in_=sgc[blk & 1]), "sdo", 16)
                block.sync(_)
                sgl_tok[blk & 1] = mk.tok("sdo")
            else:
                def _(sync, blk=blk):
                    if sgl_tok[blk & 1] > 0:
                        mk.wait(sync, "tv", sgl_tok[blk & 1])
                    mk.inc(sync.dma_start(out=sgc[blk & 1],
                                          in_=e["sg_park"][blk * 128:(blk + 1) * 128, :]), "sd0", 16)
                block.sync(_)
                sgld = mk.tok("sd0")

                def _(vector, blk=blk, psum=psum, mm_done=mm_done, sgld=sgld, sctok=mk.tok("sd0")):
                    mk.wait(vector, "tt", mm_done)
                    mk.wait(vector, "sd0", max(sgld, sctok))
                    vector.scalar_tensor_tensor(out=gA, in0=psum[:], scalar=e["s_swu"][:, blk:blk + 1],
                                                in1=e["scB"][:], op0=ALU.mult, op1=ALU.mult)
                    vector.tensor_tensor(out=gA, in0=gA, in1=sgc[blk & 1], op=ALU.mult)
                    vector.tensor_scalar(out=av(AT, blk * 1024, 1024, BF16), in0=gA,
                                         scalar1=-10.0, scalar2=10.0, op0=ALU.max, op1=ALU.min)
                    mk.inc(vector.engine_nop(), "tv", 1)
                block.vector(_)
                evac_tok[blk & 1] = mk.tok("tv")
                sgl_tok[blk & 1] = mk.tok("tv")
                if filler:
                    want = max(1, (len(filler) + 15 - blk) // (16 - blk))
                    while want > 0 and filler:
                        if filler_popped[0] < filler_n_p1 and filler_popped[0] >= (blk + 1) // 2:
                            break
                        filler.popleft()()
                        filler_popped[0] += 1
                        want -= 1

    from collections import deque
    xs_g = av(AT, 0, 1024, F32)
    xs_g2 = av(AT, 2048, 1024, F32)
    iT = [av(AT, c * 1024, 1024, BF16) for c in range(16)]
    xs_i = av(AG, 0, 1024, F32)
    xs_i2 = av(AG, 2048, 1024, F32)

    qg_units = build_q_units(mk, block, e, bT, bq, e["s_invg"], 32, half_scale=False,
                             xs_tmp=xs_g, row_off=0,
                             deps=[("sq", tl_tok), ("sdo", st_tok), ("tsc", mk.tok("tsc"))])
    emit_units(qg_units)
    qg_tok = mk.tok("tv")
    emit_scB_load(mk, block, e, row_off=0)
    if e.get("dbg"):
        def _(sync):
            mk.wait(sync, "tv", qg_tok)
            for c in range(32):
                mk.inc(sync.dma_start(out=e["dbg_qg"][c * 128:(c + 1) * 128, :], in_=bq[c]), "sdo", 16)
            mk.inc(sync.dma_start(out=e["dbg_q2"], in_=e["qtmp"][:, 0:3072]), "sdo", 16)
            mk.inc(sync.dma_start(out=e["dbg_q3"], in_=e["qsc"][:]), "sdo", 16)
        block.sync(_)
    qu_units = deque(build_q_units(mk, block, e, bT, bT, e["s_invu"], 32, half_scale=False,
                                   xs_tmp=xs_g, row_off=2048, deps=[("tv", qg_tok)]))
    mm_block(e["wg_t"], e["s_swg"], qg_tok, is_gate=True, filler=qu_units)
    while qu_units:
        qu_units.popleft()()
    qu_tok = mk.tok("tv")
    emit_scB_load(mk, block, e, row_off=2048)
    if hook_after_gate is not None:
        hook_after_gate()
    qi_units = deque(build_q_units(mk, block, e, iT, iT, e["s_invi"], 16, half_scale=True,
                                   xs_tmp=xs_i, row_off=4096, deps=[]))
    mm_block(e["wu_t"], e["s_swu"], qu_tok, is_gate=False, filler=qi_units, qbuf=bT)
    while qi_units:
        qi_units.popleft()()
    qi_tok = mk.tok("tv")
    if e.get("dbg"):
        def _(sync):
            mk.wait(sync, "tv", mk.tok("tv"))
            mk.wait(sync, "tt", mk.tok("tt"))
            for c in range(16):
                mk.inc(sync.dma_start(out=e["dbg_it"][c * 128:(c + 1) * 128, :], in_=iT[c]), "sdo", 16)
        block.sync(_)
    if hook_after_up is not None:
        hook_after_up()

    barrier(mk, block)
    swdB = av(AT, 16384, 4096, F32)
    wdrow = av(AG, 24576, 4096, F32, 1, 0)

    def _(sync):
        mk.inc(sync.dma_start(out=wdrow, in_=e["s_wd"]), "sd0", 16)
    block.sync(_)
    wdr_tok = mk.tok("sd0")

    def _(tensor):
        mk.wait(tensor, "sd0", wdr_tok)
        for h in range(8):
            mk.inc(tensor.matmul(p512[h % 4][:, 0:512] if h < 4 else (pbig0 if h < 6 else pbig1)[:, 512 * (h & 1):512 * (h & 1) + 512],
                                 e["onesr"], wdrow[0:1, 512 * h:512 * h + 512],
                                 start=True, stop=True), "tt", 1)
    block.tensor(_)
    wdb_tok = mk.tok("tt")

    def _(vector):
        mk.wait(vector, "tt", wdb_tok)
        for h in range(8):
            srcp = p512[h % 4][:, 0:512] if h < 4 else (pbig0 if h < 6 else pbig1)[:, 512 * (h & 1):512 * (h & 1) + 512]
            vector.tensor_copy(out=swdB[:, 512 * h:512 * h + 512], in_=srcp)
        mk.inc(vector.engine_nop(), "tv", 1)
    block.vector(_)
    swdB_tok = mk.tok("tv")

    wdsl = [av(AG, 0, 8192, BF16), av(AG, 8192, 8192, BF16)]
    xe = [av(AG, 16384 + k * 1024, 512, F32) for k in range(4)]
    res = [av(AG, 20480 + k * 1024, 512, F32) for k in range(4)]

    def bank(r8):
        if r8 < 4:
            return p512[r8][:, 0:512]
        b = r8 - 4
        return (pbig0 if b < 2 else pbig1)[:, 512 * (b & 1):512 * (b & 1) + 512]

    wd_consumed = [0, 0]
    ev_done = {}
    xe_used = [0, 0, 0, 0]
    res_st = [0, 0, 0, 0]
    for hc in range(8):
        slot = hc & 1
        sem = "sw0" if slot == 0 else "sw1"

        def _(gpsimd, hc=hc, slot=slot, sem=sem):
            if wd_consumed[slot] > 0:
                mk.wait(gpsimd, "tt", wd_consumed[slot])
            if hc == 0:
                mk.wait(gpsimd, "tv", qi_tok)
            mk.inc(gpsimd.dma_start(out=wdsl[slot],
                                    in_=e["wd_t"][hc].rearrange("p a b -> p (a b)")), sem, 16)
        block.gpsimd(_)
        wtok = mk.tok(sem)

        for r8h in range(2):
            def _(tensor, hc=hc, slot=slot, sem=sem, wtok=wtok, r8h=r8h):
                mk.wait(tensor, sem, wtok)
                if hc == 0 and r8h == 0:
                    mk.wait(tensor, "tv", qi_tok)
                    mk.wait(tensor, "tv", swdB_tok)
                for r8 in range(4 * r8h, 4 * r8h + 4):
                    prev = ev_done.get(r8)
                    if prev is not None:
                        mk.wait(tensor, "tv", prev)
                for i in range(16):
                    for r8 in range(4 * r8h, 4 * r8h + 4):
                        ins = tensor.matmul(
                            bank(r8), iT[i][:, r8 * 128:r8 * 128 + 128],
                            wdsl[slot][:, i * 512:i * 512 + 512],
                            start=(i == 0), stop=(i == 15), skip_group_check=True)
                mk.inc(ins, "tt", 1)
            block.tensor(_)
            mmd = mk.tok("tt")
            if r8h == 1:
                wd_consumed[slot] = mmd

            for r8 in range(4 * r8h, 4 * r8h + 4):
                p = (hc * 8 + r8) & 3
                sem2 = "sd0" if (p & 1) == 0 else "sd1"

                def _(sync, hc=hc, r8=r8, p=p, sem2=sem2):
                    if xe_used[p] > 0:
                        mk.wait(sync, "tv", xe_used[p])
                    mk.inc(sync.dma_start(out=xe[p],
                                          in_=src[r8 * 128:(r8 + 1) * 128, hc * 512:(hc + 1) * 512]), sem2, 16)
                block.sync(_)
                xet = mk.tok(sem2)

                def _(vector, hc=hc, r8=r8, p=p, sem2=sem2, xet=xet, mmd=mmd):
                    mk.wait(vector, "tt", mmd)
                    mk.wait(vector, sem2, xet)
                    mk.wait(vector, "tv", swdB_tok)
                    if res_st[p] > 0:
                        mk.wait(vector, "sdo", res_st[p])
                    vector.scalar_tensor_tensor(out=res[p], in0=bank(r8),
                                                scalar=qsc[:, 24 + r8:25 + r8],
                                                in1=swdB[:, hc * 512:(hc + 1) * 512],
                                                op0=ALU.mult, op1=ALU.mult)
                    vector.tensor_tensor(out=res[p], in0=res[p], in1=xe[p], op=ALU.add)
                    if collect_stats:
                        vector.tensor_scalar(out=e["scB"][:, 0:512], in0=res[p], scalar1=1.0,
                                             scalar2=None, op0=ALU.mult, op1=ALU.add,
                                             accum_out=e["pS"][:, 8 * r8 + hc:8 * r8 + hc + 1])
                        vector.scalar_tensor_tensor(out=e["scB"][:, 512:1024], in0=res[p], scalar=1.0,
                                                    in1=res[p], op0=ALU.mult, op1=ALU.mult,
                                                    accum_out=e["pQ"][:, 8 * r8 + hc:8 * r8 + hc + 1])
                    mk.inc(vector.engine_nop(), "tv", 1)
                    xe_used[p] = mk.tok("tv")
                block.vector(_)
                ev_done[r8] = mk.tok("tv")
                evt = mk.tok("tv")

                def _(sync, hc=hc, r8=r8, p=p, evt=evt):
                    mk.wait(sync, "tv", evt)
                    mk.inc(sync.dma_start(out=dst[r8 * 128:(r8 + 1) * 128, hc * 512:(hc + 1) * 512],
                                          in_=res[p]), "sdo", 16)
                block.sync(_)
                res_st[p] = mk.tok("sdo")


def phase_ssm(mk, block, e, src, dst, stats_in=False, collect_stats=False):
    nc = mk.nc
    av, AXr, AG, AT = e["av"], e["AX"], e["AG"], e["AT"]
    p512, pbig0, pbig1 = e["p512"], e["pbig0"], e["pbig1"]
    stats = e["stats"]

    barrier(mk, block)
    st_tok = ln_pass(mk, block, e, src, rev=True, dscale=True, stats_in=stats_in)
    tl_tok = tload(mk, block, e, AXr, st_tok)
    uT = [av(AXr, c * 1024, 1024, BF16) for c in range(32)]

    tsl = [[av(AG, (par * 32 + s) * 512, 512, BF16) for s in range(32)] for par in range(2)]
    xe = [av(AT, k * 1024, 512, F32) for k in range(4)]
    du = av(AT, 4096, 512, F32)
    res = [av(AT, 5120 + k * 1024, 512, F32) for k in range(4)]

    def bank(r8):
        if r8 < 4:
            return p512[r8][:, 0:512]
        b = r8 - 4
        return (pbig0 if b < 2 else pbig1)[:, 512 * (b & 1):512 * (b & 1) + 512]

    t_used = [0, 0]
    ev_done = {}
    xe_used = [0, 0, 0, 0]
    res_st = [0, 0, 0, 0]
    for tc in range(8):
        par = tc & 1
        s0 = max(0, 28 - 4 * tc)
        sigmas = list(range(s0, 32))

        def _(scalar, tc=tc, par=par, sigmas=sigmas, s0=s0):
            if t_used[par] > 0:
                mk.wait(scalar, "tt", t_used[par])
            if tc == 0:
                mk.wait(scalar, "sq", tl_tok)
            n = len(sigmas)
            srcap = bass.AP(e["buf3"].tensor, 1 + 128 * s0 + 512 * tc,
                            [[1, 128], [128, n], [1, 512]])
            dst = e["arena"][:, AG + (par * 32 + s0) * 512: AG + (par * 32 + s0) * 512 + n * 512]
            dst = dst.rearrange("p (n t) -> p n t", t=512)
            mk.inc(scalar.dma_start(out=dst, in_=srcap), "sq", 16)
        block.scalar(_)
        tld = mk.tok("sq")

        for r8h in range(2):
            def _(tensor, tc=tc, par=par, sigmas=sigmas, tld=tld, r8h=r8h):
                mk.wait(tensor, "sq", tld)
                for r8 in range(4 * r8h, 4 * r8h + 4):
                    prev = ev_done.get(r8)
                    if prev is not None:
                        mk.wait(tensor, "tv", prev)
                for si, s in enumerate(sigmas):
                    for r8 in range(4 * r8h, 4 * r8h + 4):
                        ins = tensor.matmul(
                            bank(r8), uT[s][:, r8 * 128:r8 * 128 + 128],
                            tsl[par][s],
                            start=(si == 0), stop=(si == len(sigmas) - 1),
                            skip_group_check=True)
                mk.inc(ins, "tt", 1)
            block.tensor(_)
            mmd = mk.tok("tt")
            if r8h == 1:
                t_used[par] = mmd

            for r8 in range(4 * r8h, 4 * r8h + 4):
                p = (tc * 8 + r8) & 3
                sem2 = "sd0" if (p & 1) == 0 else "sd1"

                def _(sync, tc=tc, r8=r8, p=p, sem2=sem2):
                    if xe_used[p] > 0:
                        mk.wait(sync, "tv", xe_used[p])
                    mk.inc(sync.dma_start(out=xe[p],
                                          in_=src[r8 * 128:(r8 + 1) * 128, tc * 512:(tc + 1) * 512]), sem2, 16)
                block.sync(_)
                xet = mk.tok(sem2)

                def _(vector, tc=tc, r8=r8, p=p, sem2=sem2, xet=xet, mmd=mmd):
                    mk.wait(vector, "tt", mmd)
                    mk.wait(vector, sem2, xet)
                    if res_st[p] > 0:
                        mk.wait(vector, "sdo", res_st[p])
                    vector.tensor_scalar(out=du, in0=xe[p], scalar1=stats[:, 40 + r8:41 + r8],
                                         scalar2=stats[:, 48 + r8:49 + r8], op0=ALU.mult, op1=ALU.add)
                    vector.scalar_tensor_tensor(out=res[p], in0=bank(r8), scalar=1.0,
                                                in1=du, op0=ALU.mult, op1=ALU.add)
                    if collect_stats:
                        vector.tensor_scalar(out=e["scB"][:, 0:512], in0=res[p], scalar1=1.0,
                                             scalar2=None, op0=ALU.mult, op1=ALU.add,
                                             accum_out=e["pS"][:, 8 * r8 + tc:8 * r8 + tc + 1])
                        vector.scalar_tensor_tensor(out=e["scB"][:, 512:1024], in0=res[p], scalar=1.0,
                                                    in1=res[p], op0=ALU.mult, op1=ALU.mult,
                                                    accum_out=e["pQ"][:, 8 * r8 + tc:8 * r8 + tc + 1])
                    mk.inc(vector.engine_nop(), "tv", 1)
                    xe_used[p] = mk.tok("tv")
                block.vector(_)
                ev_done[r8] = mk.tok("tv")
                evt = mk.tok("tv")

                def _(sync, tc=tc, r8=r8, p=p, evt=evt):
                    mk.wait(sync, "tv", evt)
                    mk.inc(sync.dma_start(out=dst[r8 * 128:(r8 + 1) * 128, tc * 512:(tc + 1) * 512],
                                          in_=res[p]), "sdo", 16)
                block.sync(_)
                res_st[p] = mk.tok("sdo")


_NC = None

def kernel(**inputs):
    global _NC
    from concourse.bass_utils import run_bass_kernel_spmd
    per_core = host_prep(inputs)
    if _NC is None:
        _NC = build(n_ffn_blocks=2, do_ssm=True, dbg=False)
    res = run_bass_kernel_spmd(_NC, per_core, core_ids=list(range(8)))
    out = np.concatenate([res.results[k]["out"] for k in range(8)], axis=0)
    return out.reshape(2, 4096, 4096)


# version 10 (final)
# speedup vs baseline: 1.0234x; 1.0234x over previous
import numpy as np
import ml_dtypes
import concourse.bass as bass
import concourse.mybir as mybir

F32 = mybir.dt.float32
BF16 = mybir.dt.bfloat16
I32 = mybir.dt.int32
AF = mybir.ActivationFunctionType
ALU = mybir.AluOpType
AX = mybir.AxisListType

R = 1024
D = 4096
NI = 2048
EPS = 1e-5
LN2C = float(np.log(2.0))


def host_prep(inputs):
    u = np.asarray(inputs["L0_attn_u"], np.float32).reshape(8192, 4096)
    wg = np.asarray(inputs["L0_ffn_w_gate"], np.float32)
    wu = np.asarray(inputs["L0_ffn_w_up"], np.float32)
    wd = np.asarray(inputs["L0_ffn_w_down"], np.float32)
    wgt = np.ascontiguousarray(wg.reshape(16, 128, 32, 128).transpose(0, 3, 2, 1))
    wut = np.ascontiguousarray(wu.reshape(16, 128, 32, 128).transpose(0, 3, 2, 1))
    wdt = np.ascontiguousarray(wd.reshape(8, 512, 16, 128).transpose(0, 3, 2, 1))
    shared = {
        "wg_t": wgt, "wu_t": wut, "wd_t": wdt,
        "s_wg": np.ascontiguousarray(np.asarray(inputs["L0_ffn_s_wgate"], np.float32).reshape(16, 128).T),
        "s_wu": np.ascontiguousarray(np.asarray(inputs["L0_ffn_s_wup"], np.float32).reshape(16, 128).T),
        "s_wd": np.asarray(inputs["L0_ffn_s_wdown"], np.float32).reshape(1, 4096),
        "inv_g": np.ascontiguousarray(np.asarray(inputs["L0_ffn_inv_gate"], np.float32).reshape(32, 128).T),
        "inv_u": np.ascontiguousarray(np.asarray(inputs["L0_ffn_inv_up"], np.float32).reshape(32, 128).T),
        "inv_i": np.ascontiguousarray(np.asarray(inputs["L0_ffn_inv_inter"], np.float32).reshape(16, 128).T),
        "lam_row": np.asarray(inputs["L0_attn_Lambda_ri"], np.float32).reshape(1, 128),
        "p_row": np.asarray(inputs["L0_attn_p_ri"], np.float32).reshape(1, 128),
        "q_row": np.asarray(inputs["L0_attn_q_ri"], np.float32).reshape(1, 128),
        "ct_row": np.asarray(inputs["L0_attn_Ct_ri"], np.float32).reshape(1, 128),
        "vc_nat": np.asarray(inputs["L0_attn_Vc_ri"], np.float32).reshape(64, 128),
        "bp_col": np.asarray(inputs["L0_attn_Bp"], np.float32),
        "dd": np.asarray(inputs["L0_attn_D"], np.float32).reshape(1, 1),
        "log_step": np.asarray(inputs["L0_attn_log_step"], np.float32).reshape(1, 1),
    }
    per_core = []
    for k in range(8):
        m = dict(shared)
        m["x_in"] = np.ascontiguousarray(u[k * R:(k + 1) * R])
        per_core.append(m)
    return per_core


def consts():
    c = {}
    c["ident16"] = np.eye(128, dtype=ml_dtypes.bfloat16)
    c["nident16"] = (-np.eye(128)).astype(ml_dtypes.bfloat16)
    c["ident32"] = np.eye(128, dtype=np.float32)
    c["ones_row"] = np.ones((1, 128), np.float32)
    p_idx = np.arange(128)[:, None]
    c_idx = np.arange(32)[None, :]
    jtab = (c_idx + 32 * (p_idx >= 64)) + 64 * (p_idx % 64)
    Om = np.exp(-2j * np.pi * (np.arange(4096) / 4096.0))
    T1 = (1.0 - Om) / (1.0 + Om)
    cc = 2.0 / (1.0 + Om)
    c["t1r"] = T1.real[jtab].astype(np.float32)
    c["t1i"] = T1.imag[jtab].astype(np.float32)
    c["ccr"] = cc.real[jtab].astype(np.float32)
    c["cci"] = cc.imag[jtab].astype(np.float32)
    F64 = np.exp(-2j * np.pi * np.outer(np.arange(64), np.arange(64)) / 64.0)
    c["f64c"] = np.vstack([F64.real, F64.real]).astype(np.float32)
    c["f64s"] = np.vstack([F64.imag, F64.imag]).astype(np.float32)
    TW = np.exp(-2j * np.pi * np.outer(np.arange(64), np.arange(64)) / 4096.0)
    c["twc"] = TW.real.astype(np.float32)
    c["tws"] = TW.imag.astype(np.float32)
    c["f64sn"] = np.vstack([-F64.imag, -F64.imag]).astype(np.float32)
    c["f64c2"] = (F64.real / 4096.0).astype(np.float32)
    c["f64s2n"] = (-F64.imag / 4096.0).astype(np.float32)
    return c


class MK:

    def __init__(self, nc):
        self.nc = nc
        self.sem = {}
        self.cnt = {}

    def mksem(self, stack, *names):
        for n in names:
            self.sem[n] = stack.enter_context(self.nc.semaphore(n))
            self.cnt[n] = 0

    def inc(self, instr, name, amt):
        instr.then_inc(self.sem[name], amt)
        self.cnt[name] += amt
        return self.cnt[name]

    def tok(self, name):
        return self.cnt[name]

    def wait(self, eng, name, tok=None):
        eng.wait_ge(self.sem[name], self.cnt[name] if tok is None else tok)

    def guard(self, eng, name="tv"):
        ins = eng.engine_nop() if hasattr(eng, "engine_nop") else eng.nop()
        t = self.inc(ins, name, 1)
        eng.wait_ge(self.sem[name], t)


def build(n_ffn_blocks=2, do_ssm=True, dbg=False):
    nc = bass.Bass("TRN2", target_bir_lowering=False)
    mk = MK(nc)
    CST = consts()

    x_in = nc.dram_tensor("x_in", [R, D], F32, kind="ExternalInput").ap()
    wg_t = nc.dram_tensor("wg_t", [16, 128, 32, 128], F32, kind="ExternalInput").ap()
    wu_t = nc.dram_tensor("wu_t", [16, 128, 32, 128], F32, kind="ExternalInput").ap()
    wd_t = nc.dram_tensor("wd_t", [8, 128, 16, 512], F32, kind="ExternalInput").ap()
    s_wg = nc.dram_tensor("s_wg", [128, 16], F32, kind="ExternalInput").ap()
    s_wu = nc.dram_tensor("s_wu", [128, 16], F32, kind="ExternalInput").ap()
    s_wd = nc.dram_tensor("s_wd", [1, 4096], F32, kind="ExternalInput").ap()
    inv_g = nc.dram_tensor("inv_g", [128, 32], F32, kind="ExternalInput").ap()
    inv_u = nc.dram_tensor("inv_u", [128, 32], F32, kind="ExternalInput").ap()
    inv_i = nc.dram_tensor("inv_i", [128, 16], F32, kind="ExternalInput").ap()
    lam_row = nc.dram_tensor("lam_row", [1, 128], F32, kind="ExternalInput").ap()
    p_row = nc.dram_tensor("p_row", [1, 128], F32, kind="ExternalInput").ap()
    q_row = nc.dram_tensor("q_row", [1, 128], F32, kind="ExternalInput").ap()
    ct_row = nc.dram_tensor("ct_row", [1, 128], F32, kind="ExternalInput").ap()
    vc_nat = nc.dram_tensor("vc_nat", [64, 128], F32, kind="ExternalInput").ap()
    bp_col = nc.dram_tensor("bp_col", [64, 1], F32, kind="ExternalInput").ap()
    dd_in = nc.dram_tensor("dd", [1, 1], F32, kind="ExternalInput").ap()
    log_step = nc.dram_tensor("log_step", [1, 1], F32, kind="ExternalInput").ap()
    out = nc.dram_tensor("out", [R, D], F32, kind="ExternalOutput").ap()
    kindi = "ExternalOutput" if dbg else "Internal"

    x_work = nc.dram_tensor("x_work", [R, D], F32, kind=kindi).ap()
    bounce = nc.dram_tensor("bounce", [R, D], BF16, kind=kindi).ap()
    sg_park = nc.dram_tensor("sg_park", [NI, R], BF16, kind=kindi).ap()
    buf3 = nc.dram_tensor("buf3", [1, 8192], BF16, kind=kindi).ap()
    row_sc = nc.dram_tensor("row_sc", [1, 8192], F32, kind=kindi).ap()
    dbg_qg = nc.dram_tensor("dbg_qg", [D, R], BF16, kind=kindi).ap() if dbg else None
    dbg_it = nc.dram_tensor("dbg_it", [NI, R], BF16, kind=kindi).ap() if dbg else None
    dbg_st = nc.dram_tensor("dbg_st", [128, 64], F32, kind=kindi).ap() if dbg else None
    dbg_q2 = nc.dram_tensor("dbg_q2", [128, 3072], BF16, kind=kindi).ap() if dbg else None
    dbg_q3 = nc.dram_tensor("dbg_q3", [128, 48], F32, kind=kindi).ap() if dbg else None

    ic16 = nc.inline_tensor(CST["ident16"], "ic16").ap()
    ic32 = nc.inline_tensor(CST["ident32"], "ic32").ap()
    ones_row = nc.inline_tensor(CST["ones_row"], "ones_row").ap()
    t1r_c = nc.inline_tensor(CST["t1r"], "t1r").ap()
    t1i_c = nc.inline_tensor(CST["t1i"], "t1i").ap()
    ccr_c = nc.inline_tensor(CST["ccr"], "ccr").ap()
    cci_c = nc.inline_tensor(CST["cci"], "cci").ap()
    f64c_c = nc.inline_tensor(CST["f64c"], "f64c").ap()
    f64s_c = nc.inline_tensor(CST["f64s"], "f64s").ap()
    f64sn_c = nc.inline_tensor(CST["f64sn"], "f64sn").ap()
    twc_c = nc.inline_tensor(CST["twc"], "twc").ap()
    tws_c = nc.inline_tensor(CST["tws"], "tws").ap()
    f64c2_c = nc.inline_tensor(CST["f64c2"], "f64c2").ap()
    f64s2n_c = nc.inline_tensor(CST["f64s2n"], "f64s2n").ap()

    from contextlib import ExitStack
    with ExitStack() as stack:
        sbt = lambda n, s, d: stack.enter_context(nc.sbuf_tensor(n, s, d))
        arena = sbt("arena", [128, 92176], BF16)
        AX_, AG, AT = 0, 32768, 65536
        ident16 = sbt("ident16", [128, 128], BF16).ap()
        ident32 = sbt("ident32", [128, 128], F32).ap()
        onesr = sbt("onesr", [1, 128], F32).ap()
        s_invg = sbt("s_invg", [128, 32], F32)
        s_invu = sbt("s_invu", [128, 32], F32)
        s_invi = sbt("s_invi", [128, 16], F32)
        s_swg = sbt("s_swg", [128, 16], F32)
        s_swu = sbt("s_swu", [128, 16], F32)
        stats = sbt("stats", [128, 80], F32)
        pS = sbt("pS", [128, 64], F32)
        pQ = sbt("pQ", [128, 64], F32)
        qsc = sbt("qsc", [128, 48], F32)
        qtmp = sbt("qtmp", [128, 6144], BF16)
        qrow = sbt("qrow", [16, 128], F32)
        wslab = sbt("wslab", [128, 2, 2048], BF16)
        ktile = sbt("ktile", [64, 64], BF16)
        scB = sbt("scB", [128, 1024], F32)

        pbig0 = stack.enter_context(nc.psum_tensor("pbig0", [128, 1024], F32))
        pbig1 = stack.enter_context(nc.psum_tensor("pbig1", [128, 1024], F32))
        p512 = [stack.enter_context(nc.psum_tensor(f"p512_{i}", [128, 512], F32)) for i in range(4)]

        mk.mksem(stack, "sd0", "sd1", "sw0", "sw1", "sq", "sgq", "tv", "tsc", "tt", "sdo")
        block = stack.enter_context(nc.Block())

        def av(base, off, n, dtype=BF16, parts=128, p0=0):
            if dtype == BF16:
                ap = arena[p0:p0 + parts, base + off:base + off + n]
            elif dtype in (F32, I32):
                ap = arena[p0:p0 + parts, base + off:base + off + 2 * n].bitcast(dtype)
            else:
                raise ValueError(dtype)
            return ap

        @block.sync
        def _(sync):
            for dst, src in [(ident16[:], ic16), (ident32[:], ic32), (onesr[:], ones_row),
                             (s_invg[:], inv_g), (s_invu[:], inv_u), (s_invi[:], inv_i),
                             (s_swg[:], s_wg), (s_swu[:], s_wu)]:
                mk.inc(sync.dma_start(out=dst, in_=src), "sdo", 16)

        init_tok = mk.tok("sdo")

        kenv = dict(
            av=av, AG=AG, AT=AT, onesr=onesr, ident32=ident32, ktile=ktile,
            pbig0=pbig0, pbig1=pbig1, p512=p512, buf3=buf3,
            lam_row=lam_row, p_row=p_row, q_row=q_row, ct_row=ct_row,
            vc_nat=vc_nat, bp_col=bp_col, log_step=log_step,
            t1r=t1r_c, t1i=t1i_c, ccr=ccr_c, cci=cci_c,
            f64c=f64c_c, f64s=f64s_c, f64sn=f64sn_c, twc=twc_c, tws=tws_c,
            f64c2=f64c2_c, f64s2n=f64s2n_c, init_tok=init_tok)
        if do_ssm:
            KL = k_layout(kenv)
            phase_k_a(mk, block, kenv, KL)
            hooks = dict(
                hook_after_gate=lambda: phase_k_b(mk, block, kenv, KL),
                hook_after_up=lambda: phase_k_c(mk, block, kenv, KL, None))
        else:
            hooks = dict()

        env = dict(av=av, AX=AX_, AG=AG, AT=AT, arena=arena,
                   onesr=onesr, ident16=ident16, ident32=ident32,
                   s_invg=s_invg, s_invu=s_invu, s_invi=s_invi,
                   s_swg=s_swg, s_swu=s_swu, s_wd=s_wd,
                   stats=stats, pS=pS, pQ=pQ, qsc=qsc, qtmp=qtmp, qrow=qrow, wslab=wslab,
                   scB=scB,
                   pbig0=pbig0, pbig1=pbig1, p512=p512,
                   wg_t=wg_t, wu_t=wu_t, wd_t=wd_t,
                   bounce=bounce, sg_park=sg_park, row_sc=row_sc,
                   buf3=buf3, dd_in=dd_in, ktile=ktile,
                   dbg_qg=dbg_qg, dbg_it=dbg_it, dbg_st=dbg_st, dbg_q2=dbg_q2, dbg_q3=dbg_q3, dbg=dbg)

        if n_ffn_blocks >= 1:
            phase_ffn(mk, block, env, src=x_in,
                      dst=(x_work if (do_ssm or n_ffn_blocks >= 2) else out),
                      collect_stats=do_ssm, **hooks)
        if do_ssm:
            phase_ssm(mk, block, env, src=x_work, dst=(x_work if n_ffn_blocks >= 2 else out),
                      stats_in=True, collect_stats=(n_ffn_blocks >= 2))
        if n_ffn_blocks >= 2:
            phase_ffn(mk, block, env, src=x_work, dst=out, stats_in=True)

        @block.sync
        def _(sync):
            mk.wait(sync, "sdo")
            mk.wait(sync, "sd0")
            mk.wait(sync, "sd1")

    return nc


def k_layout(e):
    av, AG, AT = e["av"], e["AG"], e["AT"]
    KR = lambda off, n, parts=128, p0=0: av(AG, 2 * off, n, F32, parts, p0)
    TTL = lambda off, n, parts=128, p0=0: av(AT, 2 * off, n, F32, parts, p0)
    L = {}
    B0 = 12288
    L["vB"] = [TTL(B0 + 64 * i, 64) for i in range(8)]
    L["nLr"] = TTL(B0 + 512, 64); L["nLi"] = TTL(B0 + 576, 64)
    L["gr"] = TTL(B0 + 640, 32); L["gi"] = TTL(B0 + 672, 32)
    L["t1r"] = TTL(B0 + 704, 32); L["t1i"] = TTL(B0 + 736, 32)
    L["ccr"] = TTL(B0 + 768, 32); L["cci"] = TTL(B0 + 800, 32)
    L["s2b"] = TTL(B0 + 832, 1)
    L["f64c"] = TTL(B0 + 840, 64, 128)
    L["f64s"] = TTL(B0 + 904, 64, 128)
    L["f64sn"] = TTL(B0 + 968, 64, 128)
    L["twc"] = KR(15872, 64, 64)
    L["tws"] = KR(15936, 64, 64)
    L["f64c2"] = KR(16000, 64, 64)
    L["f64s2n"] = KR(16064, 64, 64)
    L["lamr"] = KR(0, 128, 1, 0); L["pr"] = KR(128, 128, 1, 0)
    L["qr"] = KR(256, 128, 1, 0); L["ctr"] = KR(384, 128, 1, 0)
    L["Brow"] = KR(512, 64, 1, 0); L["Birow"] = KR(576, 64, 1, 0)
    L["vrow"] = [KR(640 + 64 * i, 64, 1, 0) for i in range(8)]
    L["tA"] = KR(1152, 64, 1, 0); L["tB"] = KR(1216, 64, 1, 0)
    L["lsrow"] = KR(1280, 1, 1, 0); L["s2row"] = KR(1281, 1, 1, 0)
    L["s2tmp"] = KR(1284, 1, 1, 0)
    L["vcn"] = KR(1536, 128, 64)
    L["vcTr"] = KR(1664, 64, 64); L["vcTi"] = KR(1728, 64, 64)
    L["vcr_s"] = KR(1792, 64, 64); L["vci_s"] = KR(1856, 64, 64)
    L["bp"] = KR(1920, 1, 64)
    L["kx"] = [KR(14336 + 32 * i, 32) for i in range(8)]
    L["arre"] = KR(14592, 32); L["aim"] = KR(14624, 32)
    L["t1"] = KR(14656, 64); L["t2"] = KR(14720, 64)
    L["tmp"] = KR(14784, 32); L["tmp2"] = KR(14816, 32); L["tmp3"] = KR(14848, 32)
    L["arre2"] = KR(14880, 32); L["aim2"] = KR(14912, 32)
    L["KR3"] = lambda off: e["av"](e["AG"], 2 * off, 2048, F32).rearrange("p (c n) -> p c n", n=64)
    L["srT"] = KR(1024 - 1024 + 15360, 64, 64) if False else None
    L["srT"] = KR(15360, 64, 64); L["siT"] = KR(15424, 64, 64)
    L["spr"] = KR(15488, 64, 64); L["spi"] = KR(15552, 64, 64)
    L["sprT"] = KR(15616, 64, 64); L["spiT"] = KR(15680, 64, 64)
    L["tw1"] = KR(15744, 64, 64); L["tw2"] = KR(15808, 64, 64)
    return L


def phase_k_a(mk, block, e, L):
    nc = mk.nc
    p512 = e["p512"]
    ident32, onesr = e["ident32"], e["onesr"]

    @block.sync
    def _(sync):
        mk.wait(sync, "sdo", e["init_tok"])
        for dst, srcc in [(L["lamr"], e["lam_row"]), (L["pr"], e["p_row"]), (L["qr"], e["q_row"]),
                          (L["ctr"], e["ct_row"]), (L["vcn"], e["vc_nat"]), (L["bp"], e["bp_col"]),
                          (L["lsrow"], e["log_step"]),
                          (L["t1r"], e["t1r"]), (L["t1i"], e["t1i"]),
                          (L["ccr"], e["ccr"]), (L["cci"], e["cci"]),
                          (L["f64c"], e["f64c"]), (L["f64s"], e["f64s"]), (L["f64sn"], e["f64sn"])]:
            mk.inc(sync.dma_start(out=dst, in_=srcc), "sd0", 16)
    k_loads = mk.tok("sd0")

    @block.vector
    def _(vector):
        mk.wait(vector, "sd0", k_loads)
        vector.tensor_copy(out=L["vcr_s"], in_=L["vcn"][:, 0:128:2])
        vector.tensor_copy(out=L["vci_s"], in_=L["vcn"][:, 1:128:2])
        mk.inc(vector.tensor_scalar(out=L["s2tmp"], in0=L["lsrow"], scalar1=-1.0, scalar2=LN2C,
                                    op0=ALU.mult, op1=ALU.add), "tv", 1)
    t_split = mk.tok("tv")

    @block.scalar
    def _(scalar):
        mk.wait(scalar, "tv", t_split)
        mk.inc(scalar.activation(out=L["s2row"], in_=L["s2tmp"], func=AF.Exp), "tsc", 1)
    t_s2 = mk.tok("tsc")

    @block.tensor
    def _(tensor):
        mk.wait(tensor, "tv", t_split)
        mk.inc(tensor.transpose(p512[0][0:64, 0:64], L["vcr_s"], ident32[0:64, 0:64]), "tt", 1)
        mk.inc(tensor.transpose(p512[1][0:64, 0:64], L["vci_s"], ident32[0:64, 0:64]), "tt", 1)
    t_vct = mk.tok("tt")

    @block.vector
    def _(vector):
        mk.wait(vector, "tt", t_vct)
        vector.tensor_copy(out=L["vcTr"], in_=p512[0][0:64, 0:64])
        vector.tensor_copy(out=L["vcTi"], in_=p512[1][0:64, 0:64])
        mk.inc(vector.engine_nop(), "tv", 1)
    t_vce = mk.tok("tv")

    @block.tensor
    def _(tensor):
        mk.wait(tensor, "tv", t_vce)
        mk.inc(tensor.matmul(p512[0][0:1, 0:64], L["bp"], L["vcTr"], start=True, stop=True), "tt", 1)
        mk.inc(tensor.matmul(p512[1][0:1, 0:64], L["bp"], L["vcTi"], start=True, stop=True), "tt", 1)
    t_b = mk.tok("tt")

    @block.vector
    def _(vector):
        mk.wait(vector, "tt", t_b)
        vector.tensor_copy(out=L["Brow"], in_=p512[0][0:1, 0:64])
        vector.tensor_copy(out=L["Birow"], in_=p512[1][0:1, 0:64])
        mk.guard(vector)
        ctr = L["ctr"][0:1, 0:128:2]; cti = L["ctr"][0:1, 1:128:2]
        pr = L["pr"][0:1, 0:128:2]; pi = L["pr"][0:1, 1:128:2]
        qr = L["qr"][0:1, 0:128:2]; qi = L["qr"][0:1, 1:128:2]
        tA, tB = L["tA"], L["tB"]
        vrow = L["vrow"]

        def cmul_conj(vr_out, vi_out, xr, xi, yr, yi):
            vector.tensor_tensor(out=tA, in0=xr, in1=yr, op=ALU.mult)
            vector.tensor_tensor(out=tB, in0=xi, in1=yi, op=ALU.mult)
            mk.guard(vector)
            vector.tensor_tensor(out=vr_out, in0=tA, in1=tB, op=ALU.add)
            vector.tensor_tensor(out=tA, in0=xr, in1=yi, op=ALU.mult)
            vector.tensor_tensor(out=tB, in0=xi, in1=yr, op=ALU.mult)
            mk.guard(vector)
            vector.tensor_tensor(out=vi_out, in0=tA, in1=tB, op=ALU.subtract)
        cmul_conj(vrow[0], vrow[1], ctr, cti, L["Brow"], L["Birow"])
        cmul_conj(vrow[2], vrow[3], ctr, cti, pr, pi)
        cmul_conj(vrow[4], vrow[5], qr, qi, L["Brow"], L["Birow"])
        cmul_conj(vrow[6], vrow[7], qr, qi, pr, pi)
        mk.inc(vector.engine_nop(), "tv", 1)
    t_v = mk.tok("tv")

    @block.tensor
    def _(tensor):
        mk.wait(tensor, "tv", t_v)
        mk.wait(tensor, "tsc", t_s2)
        lam_re = L["lamr"][0:1, 0:128:2]; lam_im = L["lamr"][0:1, 1:128:2]
        tensor.matmul(e["p512"][3][:, 0:64], onesr, lam_re, start=True, stop=True)
        tensor.matmul(e["p512"][3][:, 64:128], onesr, lam_im, start=True, stop=True)
        mk.inc(tensor.matmul(e["p512"][3][:, 128:129], onesr, L["s2row"], start=True, stop=True), "tt", 1)
        for i in range(8):
            ins = tensor.matmul(e["p512"][2][:, 64 * i:64 * i + 64], onesr, L["vrow"][i], start=True, stop=True)
        mk.inc(ins, "tt", 1)
    t_bc = mk.tok("tt")

    @block.vector
    def _(vector):
        mk.wait(vector, "tt", t_bc)
        vector.tensor_scalar(out=L["nLr"], in0=e["p512"][3][:, 0:64], scalar1=-1.0, scalar2=None, op0=ALU.mult)
        vector.tensor_scalar(out=L["nLi"], in0=e["p512"][3][:, 64:128], scalar1=-1.0, scalar2=None, op0=ALU.mult)
        vector.tensor_copy(out=L["s2b"], in_=e["p512"][3][:, 128:129])
        for i in range(8):
            vector.tensor_copy(out=L["vB"][i], in_=e["p512"][2][:, 64 * i:64 * i + 64])
        mk.guard(vector)
        vector.tensor_scalar(out=L["gr"], in0=L["t1r"], scalar1=L["s2b"], scalar2=None, op0=ALU.mult)
        vector.tensor_scalar(out=L["gi"], in0=L["t1i"], scalar1=L["s2b"], scalar2=None, op0=ALU.mult)
        mk.inc(vector.engine_nop(), "tv", 1)
    return mk.tok("tv")


def phase_k_b(mk, block, e, L):
    @block.vector
    def _(vector):
        KR3 = L["KR3"]
        dra = KR3(2048); dia = KR3(4096); t1a = KR3(6144)
        t2a = KR3(8192); Pa = KR3(10240); Qa = KR3(12288)
        grX = L["gr"][:, :, None].broadcast_to([128, 32, 64])
        giX = L["gi"][:, :, None].broadcast_to([128, 32, 64])
        nLrX = L["nLr"][:, None, :].broadcast_to([128, 32, 64])
        nLiX = L["nLi"][:, None, :].broadcast_to([128, 32, 64])
        vector.tensor_tensor(out=dra, in0=grX, in1=nLrX, op=ALU.add)
        vector.tensor_tensor(out=dia, in0=giX, in1=nLiX, op=ALU.add)
        vector.tensor_tensor(out=t1a, in0=dra, in1=dra, op=ALU.mult)
        vector.tensor_tensor(out=t2a, in0=dia, in1=dia, op=ALU.mult)
        vector.tensor_tensor(out=t1a, in0=t1a, in1=t2a, op=ALU.add)
        vector.reciprocal(out=t2a, in_=t1a)
        vector.tensor_tensor(out=Pa, in0=dra, in1=t2a, op=ALU.mult)
        vector.tensor_tensor(out=Qa, in0=dia, in1=t2a, op=ALU.mult)
        kx = L["kx"]
        for vi_ in range(4):
            vrX = L["vB"][2 * vi_][:, None, :].broadcast_to([128, 32, 64])
            viX = L["vB"][2 * vi_ + 1][:, None, :].broadcast_to([128, 32, 64])
            vector.tensor_tensor(out=t1a, in0=Pa, in1=vrX, op=ALU.mult)
            vector.tensor_tensor(out=t2a, in0=Qa, in1=viX, op=ALU.mult)
            vector.tensor_tensor(out=t1a, in0=t1a, in1=t2a, op=ALU.add)
            vector.reduce_sum(out=kx[2 * vi_], in_=t1a, axis=AX.X)
            vector.tensor_tensor(out=t1a, in0=Pa, in1=viX, op=ALU.mult)
            vector.tensor_tensor(out=t2a, in0=Qa, in1=vrX, op=ALU.mult)
            vector.tensor_tensor(out=t1a, in0=t1a, in1=t2a, op=ALU.subtract)
            vector.reduce_sum(out=kx[2 * vi_ + 1], in_=t1a, axis=AX.X)
        mk.guard(vector)
        k00r, k00i, k01r, k01i, k10r, k10i, k11r, k11i = kx
        G = lambda: mk.guard(vector)
        t1, t2 = L["t1"], L["t2"]
        tmp, tmp2, tmp3 = L["tmp"], L["tmp2"], L["tmp3"]
        arre, aim = L["arre"], L["aim"]
        arre2, aim2 = L["arre2"], L["aim2"]
        denr = t1[:, 0:32]
        vector.tensor_scalar(out=denr, in0=k11r, scalar1=1.0, scalar2=None, op0=ALU.add)
        G()
        vector.tensor_tensor(out=tmp, in0=denr, in1=denr, op=ALU.mult)
        vector.tensor_tensor(out=tmp2, in0=k11i, in1=k11i, op=ALU.mult)
        G()
        vector.tensor_tensor(out=tmp, in0=tmp, in1=tmp2, op=ALU.add)
        G()
        vector.reciprocal(out=tmp3, in_=tmp)
        G()
        zr = t2[:, 0:32]; zi = t2[:, 32:64]
        vector.tensor_tensor(out=zr, in0=k01r, in1=k10r, op=ALU.mult)
        vector.tensor_tensor(out=tmp, in0=k01i, in1=k10i, op=ALU.mult)
        vector.tensor_tensor(out=zi, in0=k01r, in1=k10i, op=ALU.mult)
        vector.tensor_tensor(out=tmp2, in0=k01i, in1=k10r, op=ALU.mult)
        G()
        vector.tensor_tensor(out=zr, in0=zr, in1=tmp, op=ALU.subtract)
        vector.tensor_tensor(out=zi, in0=zi, in1=tmp2, op=ALU.add)
        G()
        vector.tensor_tensor(out=tmp, in0=zr, in1=denr, op=ALU.mult)
        vector.tensor_tensor(out=tmp2, in0=zi, in1=k11i, op=ALU.mult)
        G()
        vector.tensor_tensor(out=tmp, in0=tmp, in1=tmp2, op=ALU.add)
        G()
        vector.tensor_tensor(out=arre, in0=tmp, in1=tmp3, op=ALU.mult)
        vector.tensor_tensor(out=tmp, in0=zi, in1=denr, op=ALU.mult)
        vector.tensor_tensor(out=tmp2, in0=zr, in1=k11i, op=ALU.mult)
        G()
        vector.tensor_tensor(out=tmp, in0=tmp, in1=tmp2, op=ALU.subtract)
        G()
        vector.tensor_tensor(out=aim, in0=tmp, in1=tmp3, op=ALU.mult)
        G()
        vector.tensor_tensor(out=arre2, in0=k00r, in1=arre, op=ALU.subtract)
        vector.tensor_tensor(out=aim2, in0=k00i, in1=aim, op=ALU.subtract)
        G()
        vector.tensor_tensor(out=tmp, in0=L["ccr"], in1=arre2, op=ALU.mult)
        vector.tensor_tensor(out=tmp2, in0=L["cci"], in1=aim2, op=ALU.mult)
        vector.tensor_tensor(out=tmp3, in0=L["ccr"], in1=aim2, op=ALU.mult)
        vector.tensor_tensor(out=t1[:, 32:64], in0=L["cci"], in1=arre2, op=ALU.mult)
        G()
        vector.tensor_tensor(out=arre, in0=tmp, in1=tmp2, op=ALU.subtract)
        vector.tensor_tensor(out=tmp, in0=tmp3, in1=t1[:, 32:64], op=ALU.add)
        G()
        vector.tensor_scalar(out=aim, in0=tmp, scalar1=-1.0, scalar2=None, op0=ALU.mult)
        mk.inc(vector.engine_nop(), "tv", 1)
    return mk.tok("tv")


def phase_k_c(mk, block, e, L, t_at):
    nc = mk.nc
    p512 = e["p512"]
    ident32 = e["ident32"]
    ktile = e["ktile"]
    arre, aim = L["arre"], L["aim"]

    @block.sync
    def _(sync):
        mk.wait(sync, "tv", mk.tok("tv"))
        for dst, srcc in [(L["twc"], e["twc"]), (L["tws"], e["tws"]),
                          (L["f64c2"], e["f64c2"]), (L["f64s2n"], e["f64s2n"])]:
            mk.inc(sync.dma_start(out=dst, in_=srcc), "sd0", 16)
    kc_ld = mk.tok("sd0")

    @block.tensor
    def _(tensor):
        mk.wait(tensor, "tv", mk.tok("tv"))
        for h in range(2):
            pp = slice(64 * h, 64 * h + 64)
            sl = slice(32 * h, 32 * h + 32)
            tensor.matmul(p512[0][0:64, sl], L["f64c"][pp, :], arre[pp, :], start=True, stop=False)
            mk.inc(tensor.matmul(p512[0][0:64, sl], L["f64sn"][pp, :], aim[pp, :], start=False, stop=True, skip_group_check=True), "tt", 1)
            tensor.matmul(p512[1][0:64, sl], L["f64s"][pp, :], arre[pp, :], start=True, stop=False)
            mk.inc(tensor.matmul(p512[1][0:64, sl], L["f64c"][pp, :], aim[pp, :], start=False, stop=True, skip_group_check=True), "tt", 1)
    t_mm1 = mk.tok("tt")

    @block.vector
    def _(vector):
        mk.wait(vector, "tt", t_mm1)
        mk.wait(vector, "sd0", kc_ld)
        vector.tensor_copy(out=L["srT"], in_=p512[0][0:64, 0:64])
        vector.tensor_copy(out=L["siT"], in_=p512[1][0:64, 0:64])
        mk.guard(vector)
        vector.tensor_tensor(out=L["spr"], in0=L["srT"], in1=L["twc"], op=ALU.mult)
        vector.tensor_tensor(out=L["spi"], in0=L["siT"], in1=L["tws"], op=ALU.mult)
        vector.tensor_tensor(out=L["sprT"], in0=L["srT"], in1=L["tws"], op=ALU.mult)
        vector.tensor_tensor(out=L["spiT"], in0=L["siT"], in1=L["twc"], op=ALU.mult)
        mk.guard(vector)
        vector.tensor_tensor(out=L["tw1"], in0=L["spr"], in1=L["spi"], op=ALU.subtract)
        vector.tensor_tensor(out=L["tw2"], in0=L["sprT"], in1=L["spiT"], op=ALU.add)
        mk.inc(vector.engine_nop(), "tv", 1)
    t_tw = mk.tok("tv")

    @block.tensor
    def _(tensor):
        mk.wait(tensor, "tv", t_tw)
        mk.inc(tensor.transpose(p512[2][0:64, 0:64], L["tw1"], ident32[0:64, 0:64]), "tt", 1)
        mk.inc(tensor.transpose(p512[3][0:64, 0:64], L["tw2"], ident32[0:64, 0:64]), "tt", 1)
    t_spt = mk.tok("tt")

    @block.vector
    def _(vector):
        mk.wait(vector, "tt", t_spt)
        vector.tensor_copy(out=L["sprT"], in_=p512[2][0:64, 0:64])
        vector.tensor_copy(out=L["spiT"], in_=p512[3][0:64, 0:64])
        mk.inc(vector.engine_nop(), "tv", 1)
    t_spe = mk.tok("tv")

    @block.tensor
    def _(tensor):
        mk.wait(tensor, "tv", t_spe)
        tensor.matmul(p512[0][0:64, 0:64], L["f64c2"], L["sprT"], start=True, stop=False)
        mk.inc(tensor.matmul(p512[0][0:64, 0:64], L["f64s2n"], L["spiT"], start=False, stop=True, skip_group_check=True), "tt", 1)
    t_mm2 = mk.tok("tt")

    @block.vector
    def _(vector):
        mk.wait(vector, "tt", t_mm2)
        vector.tensor_copy(out=ktile[:], in_=p512[0][0:64, 0:64])
        vector.memset(e["av"](e["AG"], 0, 4096, BF16), 0.0)
        mk.inc(vector.engine_nop(), "tv", 1)
    t_kt = mk.tok("tv")

    @block.sync
    def _(sync):
        mk.wait(sync, "tv", t_kt)
        mk.inc(sync.dma_start(out=e["buf3"][0, 0:4096], in_=e["av"](e["AG"], 0, 4096, BF16, 1, 0)), "sdo", 16)
        dstap = bass.AP(e["buf3"].tensor, 4096, [[64, 64], [1, 64]])
        mk.inc(sync.dma_start(out=dstap, in_=ktile[:]), "sdo", 16)
    return mk.tok("sdo")


ALL_SEMS = ["sd0", "sd1", "sw0", "sw1", "sq", "sgq", "tv", "tsc", "tt", "sdo"]


def barrier(mk, block, engines=("sync", "vector", "scalar", "tensor", "gpsimd")):
    toks = {s: mk.tok(s) for s in ALL_SEMS}
    for en in engines:
        def _(eng, toks=toks):
            for s, v in toks.items():
                if v > 0:
                    eng.wait_ge(mk.sem[s], v)
        getattr(block, en)(_)


def ln_pass(mk, block, e, src, rev=False, dscale=False, stats_in=False):
    nc = mk.nc
    av, AG = e["av"], e["AG"]
    stats = e["stats"]
    arena = e["arena"]
    xsl = [av(AG, 0, 4096, F32), av(AG, 8192, 4096, F32)]
    x16l = [av(AG, 16384, 4096, BF16), av(AG, 20480, 4096, BF16)]
    junkA = av(AG, 24576, 4096, BF16)
    junkB = av(AG, 28672, 4096, BF16)

    if dscale:
        ddrow = av(e["AT"], 0, 1, F32, 1, 0)

        def _(sync):
            mk.inc(sync.dma_start(out=ddrow, in_=e["dd_in"]), "sd0", 16)
        block.sync(_)
        ddtok = mk.tok("sd0")

        def _(tensor):
            mk.wait(tensor, "sd0", ddtok)
            mk.wait(tensor, "tv", mk.tok("tv"))
            mk.inc(tensor.matmul(e["p512"][3][:, 0:1], e["onesr"], ddrow, start=True, stop=True), "tt", 1)
        block.tensor(_)
        ddmm = mk.tok("tt")

        def _(vector):
            mk.wait(vector, "tt", ddmm)
            vector.tensor_copy(out=stats[:, 57:58], in_=e["p512"][3][:, 0:1])
            mk.inc(vector.engine_nop(), "tv", 1)
        block.vector(_)

    if stats_in:
        def _(vector):
            G = lambda: mk.guard(vector)
            A = stats[:, 0:8]; B = stats[:, 8:16]
            M = stats[:, 16:24]; R = stats[:, 24:32]; N = stats[:, 32:40]
            D1 = stats[:, 40:48]; D2 = stats[:, 48:56]
            S = stats[:, 64:72]; S2 = stats[:, 72:80]; S3 = stats[:, 56:57]
            for r8 in range(8):
                vector.reduce_sum(out=A[:, r8:r8 + 1], in_=e["pS"][:, 8 * r8:8 * r8 + 8], axis=AX.X)
                vector.reduce_sum(out=B[:, r8:r8 + 1], in_=e["pQ"][:, 8 * r8:8 * r8 + 8], axis=AX.X)
            G()
            vector.tensor_scalar(out=M, in0=A, scalar1=1.0 / 4096.0, scalar2=None, op0=ALU.mult)
            vector.tensor_scalar(out=S, in0=B, scalar1=1.0 / 4096.0, scalar2=None, op0=ALU.mult)
            G()
            vector.tensor_tensor(out=A, in0=M, in1=M, op=ALU.mult)
            G()
            vector.tensor_tensor(out=B, in0=S, in1=A, op=ALU.subtract)
            G()
            vector.tensor_scalar(out=S, in0=B, scalar1=EPS, scalar2=None, op0=ALU.add)
            mk.inc(vector.engine_nop(), "tv", 1)
        block.vector(_)
        ve_tok = mk.tok("tv")

        def _(scalar):
            mk.wait(scalar, "tv", ve_tok)
            mk.inc(scalar.activation(out=stats[:, 0:8], in_=stats[:, 64:72], func=AF.Sqrt), "tsc", 1)
        block.scalar(_)
        sq_tok = mk.tok("tsc")

        def _(vector):
            G = lambda: mk.guard(vector)
            mk.wait(vector, "tsc", sq_tok)
            A = stats[:, 0:8]; B = stats[:, 8:16]
            M = stats[:, 16:24]; R = stats[:, 24:32]; N = stats[:, 32:40]
            D1 = stats[:, 40:48]; D2 = stats[:, 48:56]
            S = stats[:, 64:72]; S2 = stats[:, 72:80]
            vector.reciprocal(out=B, in_=A)
            G()
            vector.tensor_tensor(out=A, in0=B, in1=B, op=ALU.mult)
            G()
            vector.tensor_tensor(out=A, in0=A, in1=S, op=ALU.mult)
            G()
            vector.tensor_scalar(out=A, in0=A, scalar1=-0.5, scalar2=1.5, op0=ALU.mult, op1=ALU.add)
            G()
            vector.tensor_tensor(out=S2, in0=B, in1=A, op=ALU.mult)
            G()
            vector.tensor_tensor(out=A, in0=S2, in1=S2, op=ALU.mult)
            G()
            vector.tensor_tensor(out=A, in0=A, in1=S, op=ALU.mult)
            G()
            vector.tensor_scalar(out=A, in0=A, scalar1=-0.5, scalar2=1.5, op0=ALU.mult, op1=ALU.add)
            G()
            vector.tensor_tensor(out=R, in0=S2, in1=A, op=ALU.mult)
            G()
            vector.tensor_scalar(out=A, in0=M, scalar1=-1.0, scalar2=None, op0=ALU.mult)
            G()
            vector.tensor_tensor(out=N, in0=A, in1=R, op=ALU.mult)
            G()
            if dscale:
                ddx = stats[:, 57:58].broadcast_to([128, 8])
                vector.tensor_tensor(out=S2, in0=R, in1=ddx, op=ALU.mult)
                vector.tensor_tensor(out=D2, in0=N, in1=ddx, op=ALU.mult)
                G()
                vector.tensor_scalar(out=D1, in0=S2, scalar1=1.0, scalar2=None, op0=ALU.add)
                G()
            mk.inc(vector.engine_nop(), "tv", 1)
        block.vector(_)

        last_v_use = [0, 0]
        st_used = [0, 0]
        for r8 in range(8):
            p = r8 & 1
            sem = "sd0" if p == 0 else "sd1"

            def _(sync, r8=r8, p=p, sem=sem):
                if last_v_use[p] > 0:
                    mk.wait(sync, "tv", last_v_use[p])
                mk.inc(sync.dma_start(out=xsl[p], in_=src[r8 * 128:(r8 + 1) * 128, :]), sem, 16)
            block.sync(_)
            ld_tok = mk.tok(sem)

            def _(vector, r8=r8, p=p, sem=sem, ld_tok=ld_tok):
                mk.wait(vector, sem, ld_tok)
                if st_used[p] > 0:
                    mk.wait(vector, "sdo", st_used[p])
                rs = stats[:, 24 + r8:25 + r8]
                nmr = stats[:, 32 + r8:33 + r8]
                if rev:
                    h = x16l[p]
                    outap = bass.AP(h.tensor, h.offset + 4095, [list(h.ap[0]), [-1, 4096]])
                else:
                    outap = x16l[p]
                vector.tensor_scalar(out=outap, in0=xsl[p], scalar1=rs, scalar2=nmr,
                                     op0=ALU.mult, op1=ALU.add)
                mk.inc(vector.engine_nop(), "tv", 1)
                last_v_use[p] = mk.tok("tv")
            block.vector(_)
            x16_tok = mk.tok("tv")

            def _(sync, r8=r8, p=p, x16_tok=x16_tok):
                mk.wait(sync, "tv", x16_tok)
                mk.inc(sync.dma_start(out=e["bounce"][r8 * 128:(r8 + 1) * 128, :], in_=x16l[p]), "sdo", 16)
            block.sync(_)
            st_used[p] = mk.tok("sdo")
        return mk.tok("sdo")

    def _(vector):
        vector.memset(stats[:, 0:16], 0.0)
        mk.inc(vector.engine_nop(), "tv", 1)
    block.vector(_)
    z_tok = mk.tok("tv")

    def _(scalar):
        mk.wait(scalar, "tv", z_tok)
        scalar.copy(out=stats[:, 0:1], in_=stats[:, 0:1])
    block.scalar(_)

    last_v_use = [0, 0]
    st_used = [0, 0]
    for r8 in range(8):
        p = r8 & 1
        sem = "sd0" if p == 0 else "sd1"

        def _(sync, r8=r8, p=p, sem=sem):
            if last_v_use[p] > 0:
                mk.wait(sync, "tv", last_v_use[p])
            mk.inc(sync.dma_start(out=xsl[p], in_=src[r8 * 128:(r8 + 1) * 128, :]), sem, 16)
        block.sync(_)
        ld_tok = mk.tok(sem)

        def _(vector, r8=r8, p=p, sem=sem, ld_tok=ld_tok):
            mk.wait(vector, sem, ld_tok)
            vector.tensor_scalar(out=junkA, in0=xsl[p], scalar1=1.0, scalar2=None,
                                 op0=ALU.mult, op1=ALU.add, accum_out=stats[:, r8:r8 + 1])
            mk.inc(vector.engine_nop(), "tv", 1)
        block.vector(_)
        vsum_tok = mk.tok("tv")

        def _(scalar, r8=r8, p=p, sem=sem, ld_tok=ld_tok):
            mk.wait(scalar, sem, ld_tok)
            mk.inc(scalar.activation(out=junkB, in_=xsl[p], func=AF.Square,
                                     accum_out=stats[:, 8 + r8:9 + r8]), "tsc", 1)
        block.scalar(_)
        ssq_tok = mk.tok("tsc")

        def _(vector, r8=r8, ssq_tok=ssq_tok):
            mk.wait(vector, "tsc", ssq_tok)
            m = stats[:, 16 + r8:17 + r8]
            t0 = stats[:, 58:59]; t1 = stats[:, 59:60]; t2 = stats[:, 60:61]
            vector.tensor_scalar(out=m, in0=stats[:, r8:r8 + 1], scalar1=1.0 / 4096.0,
                                 scalar2=None, op0=ALU.mult)
            vector.tensor_scalar(out=t1, in0=stats[:, 8 + r8:9 + r8], scalar1=1.0 / 4096.0,
                                 scalar2=None, op0=ALU.mult)
            mk.guard(vector)
            vector.tensor_tensor(out=t0, in0=m, in1=m, op=ALU.mult)
            mk.guard(vector)
            vector.tensor_tensor(out=t1, in0=t1, in1=t0, op=ALU.subtract)
            mk.guard(vector)
            vector.tensor_scalar(out=t1, in0=t1, scalar1=EPS, scalar2=None, op0=ALU.add)
            mk.inc(vector.engine_nop(), "tv", 1)
        block.vector(_)
        veps_tok = mk.tok("tv")

        def _(scalar, r8=r8, veps_tok=veps_tok):
            mk.wait(scalar, "tv", veps_tok)
            mk.inc(scalar.activation(out=stats[:, 61:62], in_=stats[:, 59:60], func=AF.Sqrt), "tsc", 1)
        block.scalar(_)
        sq_tok = mk.tok("tsc")

        def _(vector, r8=r8, p=p, sq_tok=sq_tok):
            mk.wait(vector, "tsc", sq_tok)
            if st_used[p] > 0:
                mk.wait(vector, "sdo", st_used[p])
            m = stats[:, 16 + r8:17 + r8]
            rs = stats[:, 24 + r8:25 + r8]
            nmr = stats[:, 32 + r8:33 + r8]
            t1 = stats[:, 59:60]
            t2 = stats[:, 60:61]
            t3 = stats[:, 62:63]
            vector.reciprocal(out=t2, in_=stats[:, 61:62])
            mk.guard(vector)
            vector.tensor_tensor(out=t3, in0=t2, in1=t2, op=ALU.mult)
            mk.guard(vector)
            vector.tensor_tensor(out=t3, in0=t3, in1=t1, op=ALU.mult)
            mk.guard(vector)
            vector.tensor_scalar(out=t3, in0=t3, scalar1=-0.5, scalar2=1.5, op0=ALU.mult, op1=ALU.add)
            mk.guard(vector)
            vector.tensor_tensor(out=stats[:, 63:64], in0=t2, in1=t3, op=ALU.mult)
            mk.guard(vector)
            vector.tensor_tensor(out=t3, in0=stats[:, 63:64], in1=stats[:, 63:64], op=ALU.mult)
            mk.guard(vector)
            vector.tensor_tensor(out=t3, in0=t3, in1=t1, op=ALU.mult)
            mk.guard(vector)
            vector.tensor_scalar(out=t3, in0=t3, scalar1=-0.5, scalar2=1.5, op0=ALU.mult, op1=ALU.add)
            mk.guard(vector)
            vector.tensor_tensor(out=rs, in0=stats[:, 63:64], in1=t3, op=ALU.mult)
            vector.tensor_scalar(out=t3, in0=m, scalar1=-1.0, scalar2=None, op0=ALU.mult)
            mk.guard(vector)
            vector.tensor_tensor(out=nmr, in0=t3, in1=rs, op=ALU.mult)
            mk.guard(vector)
            if dscale:
                dd1 = stats[:, 57:58]
                vector.tensor_tensor(out=stats[:, 40 + r8:41 + r8], in0=rs, in1=dd1, op=ALU.mult)
                vector.tensor_tensor(out=stats[:, 48 + r8:49 + r8], in0=nmr, in1=dd1, op=ALU.mult)
                mk.guard(vector)
                vector.tensor_scalar(out=stats[:, 40 + r8:41 + r8], in0=stats[:, 40 + r8:41 + r8],
                                     scalar1=1.0, scalar2=None, op0=ALU.add)
                mk.guard(vector)
            if rev:
                h = x16l[p]
                outap = bass.AP(h.tensor, h.offset + 4095, [list(h.ap[0]), [-1, 4096]])
            else:
                outap = x16l[p]
            vector.tensor_scalar(out=outap, in0=xsl[p], scalar1=rs, scalar2=nmr,
                                 op0=ALU.mult, op1=ALU.add)
            mk.inc(vector.engine_nop(), "tv", 1)
            last_v_use[p] = mk.tok("tv")
        block.vector(_)
        x16_tok = mk.tok("tv")

        def _(sync, r8=r8, p=p, x16_tok=x16_tok):
            mk.wait(sync, "tv", x16_tok)
            mk.inc(sync.dma_start(out=e["bounce"][r8 * 128:(r8 + 1) * 128, :], in_=x16l[p]), "sdo", 16)
        block.sync(_)
        st_used[p] = mk.tok("sdo")
    return mk.tok("sdo")


def tload(mk, block, e, dst_base, store_tok):
    av = e["av"]
    base = mk.tok("sq")

    def _(scalar):
        mk.wait(scalar, "sdo", store_tok)
        for c in range(32):
            mk.inc(scalar.dma_start(
                out=av(dst_base, c * 1024, 1024, BF16),
                in_=e["bounce"][:, c * 128:(c + 1) * 128], transpose=True), "sq", 16)
    block.scalar(_)
    return mk.tok("sq"), [base + 16 * (c + 1) for c in range(32)]


def build_q_units(mk, block, e, in_chunks, dst_chunks, inv_tile, nch, half_scale,
                  xs_tmp, row_off, deps=(), chunk_sem=None, chunk_toks=None):
    nc = mk.nc
    qsc, qtmp = e["qsc"], e["qtmp"]
    mxt32 = qtmp[:, 0:2048].bitcast(F32)
    mnt32 = qtmp[:, 2048:4096].bitcast(F32)
    slot3 = qtmp[:, 4096:6144].bitcast(F32)
    tmp32 = qtmp[:, 0:2048].bitcast(I32)
    invscB32 = qtmp[:, 2048:4096].bitcast(F32)
    p512, pbig0, pbig1 = e["p512"], e["pbig0"], e["pbig1"]
    units = []
    state = {}

    for u0 in range(0, nch, 2):
        def _u(u0=u0):
            def _(vector):
                if u0 == 0:
                    for s, t in deps:
                        mk.wait(vector, s, t)
                if chunk_toks is not None:
                    mk.wait(vector, chunk_sem, chunk_toks[min(u0 + 1, nch - 1)])
                for c in range(u0, min(u0 + 2, nch)):
                    vector.tensor_scalar(out=xs_tmp, in0=in_chunks[c],
                                         scalar1=inv_tile[:, c:c + 1], scalar2=None, op0=ALU.mult)
                    if c == 0:
                        vector.tensor_copy(out=mxt32, in_=xs_tmp)
                        vector.tensor_copy(out=mnt32, in_=xs_tmp)
                    else:
                        vector.tensor_tensor(out=mxt32, in0=mxt32, in1=xs_tmp, op=ALU.max)
                        vector.tensor_tensor(out=mnt32, in0=mnt32, in1=xs_tmp, op=ALU.min)
                mk.inc(vector.engine_nop(), "tv", 1)
            block.vector(_)
        units.append(_u)

    def _u_neg():
        def _(vector):
            mk.guard(vector)
            vector.tensor_scalar(out=slot3, in0=mnt32, scalar1=-1.0, scalar2=None, op0=ALU.mult)
            mk.inc(vector.engine_nop(), "tv", 1)
        block.vector(_)
        state["tree_tok"] = mk.tok("tv")
    units.append(_u_neg)

    def bank_mx(k):
        return p512[k // 4][:, 128 * (k % 4):128 * (k % 4) + 128]

    def bank_mn(k):
        return p512[2 + k // 4][:, 128 * (k % 4):128 * (k % 4) + 128]

    def _u_tr():
        tvfull = mk.tok("tv")
        def _(tensor):
            mk.wait(tensor, "tv", max(state["tree_tok"], tvfull))
            for s, t in deps:
                mk.wait(tensor, s, t)
            for k in range(8):
                tensor.transpose(bank_mx(k), mxt32[:, 128 * k:128 * k + 128], e["ident32"])
                ins = tensor.transpose(bank_mn(k), slot3[:, 128 * k:128 * k + 128], e["ident32"])
            mk.inc(ins, "tt", 1)
        block.tensor(_)
        state["tr_tok"] = mk.tok("tt")
    units.append(_u_tr)

    def _u_sc():
        def _(vector):
            mk.wait(vector, "tt", state["tr_tok"])
            for k in range(8):
                vector.reduce_max(out=qsc[:, k:k + 1], in_=bank_mx(k), axis=AX.X)
                vector.reduce_max(out=qsc[:, 32 + k:33 + k], in_=bank_mn(k), axis=AX.X)
            mk.guard(vector)
            vector.tensor_tensor(out=qsc[:, 40:48], in0=qsc[:, 0:8], in1=qsc[:, 32:40], op=ALU.max)
            mk.guard(vector)
            vector.tensor_scalar(out=qsc[:, 8:16], in0=qsc[:, 40:48], scalar1=1.0 / 127.0,
                                 scalar2=1e-8, op0=ALU.mult, op1=ALU.max)
            mk.guard(vector)
            vector.reciprocal(out=qsc[:, 16:24], in_=qsc[:, 8:16])
            mk.guard(vector)
            if half_scale:
                vector.tensor_scalar(out=qsc[:, 24:32], in0=qsc[:, 8:16], scalar1=0.5, scalar2=None, op0=ALU.mult)
                mk.guard(vector)
            mk.inc(vector.engine_nop(), "tv", 1)
        block.vector(_)
        state["sc_tok"] = mk.tok("tv")
    units.append(_u_sc)

    def _u_qt():
        def _(tensor):
            mk.wait(tensor, "tv", state["sc_tok"])
            mk.inc(tensor.transpose(p512[2][0:16, 0:128], qsc[:, 8:24], e["ident32"]), "tt", 1)
        block.tensor(_)
        state["qt_tok"] = mk.tok("tt")
    units.append(_u_qt)

    def _u_qr():
        def _(vector):
            mk.wait(vector, "tt", state["qt_tok"])
            mk.wait(vector, "sdo", mk.tok("sdo"))
            vector.tensor_copy(out=e["qrow"][:], in_=p512[2][0:16, 0:128])
            mk.inc(vector.engine_nop(), "tv", 1)
        block.vector(_)
        state["qr_tok"] = mk.tok("tv")
    units.append(_u_qr)

    def _u_st():
        def _(sync):
            mk.wait(sync, "tv", state["qr_tok"])
            mk.inc(sync.dma_start(out=e["row_sc"][0, row_off:row_off + 1024], in_=e["qrow"][0:8, :]), "sdo", 16)
            mk.inc(sync.dma_start(out=e["row_sc"][0, row_off + 1024:row_off + 2048], in_=e["qrow"][8:16, :]), "sdo", 16)
        block.sync(_)
        state["st_tok"] = mk.tok("sdo")

        def _(gpsimd):
            mk.wait(gpsimd, "sdo", state["st_tok"])
            rep = bass.AP(e["row_sc"].tensor, row_off + 1024, [[0, 128], [1, 1024]])
            mk.inc(gpsimd.dma_start(out=invscB32[:, 0:1024], in_=rep), "sgq", 16)
        block.gpsimd(_)
        state["ivb"] = mk.tok("sgq")
    units.append(_u_st)

    for u0 in range(0, nch, 2):
        def _u(u0=u0):
            def _(vector):
                if u0 == 0:
                    mk.wait(vector, "sgq", state["ivb"])
                    mk.guard(vector)
                for c in range(u0, min(u0 + 2, nch)):
                    vector.scalar_tensor_tensor(out=tmp32[:, 0:1024], in0=in_chunks[c],
                                                scalar=inv_tile[:, c:c + 1], in1=invscB32[:, 0:1024],
                                                op0=ALU.mult, op1=ALU.mult)
                    vector.tensor_copy(out=dst_chunks[c], in_=tmp32[:, 0:1024])
                mk.inc(vector.engine_nop(), "tv", 1)
            block.vector(_)
        units.append(_u)
    return units


def emit_units(units):
    for u in units:
        u()


def emit_scB_load(mk, block, e, row_off):
    def _(sync):
        mk.wait(sync, "tv", mk.tok("tv"))
        rep = bass.AP(e["row_sc"].tensor, row_off, [[0, 128], [1, 1024]])
        mk.inc(sync.dma_start(out=e["scB"][:], in_=rep), "sd0", 16)
    block.sync(_)
    return mk.tok("sd0")


def phase_ffn(mk, block, e, src, dst, hook_after_gate=None, hook_after_up=None,
              stats_in=False, collect_stats=False):
    nc = mk.nc
    av, AXr, AG, AT = e["av"], e["AX"], e["AG"], e["AT"]
    p512, pbig0, pbig1 = e["p512"], e["pbig0"], e["pbig1"]
    wslab, qsc = e["wslab"], e["qsc"]

    barrier(mk, block)
    st_tok = ln_pass(mk, block, e, src, rev=False, dscale=False, stats_in=stats_in)
    if e.get("dbg"):
        def _(sync):
            mk.wait(sync, "tv", mk.tok("tv"))
            mk.inc(sync.dma_start(out=e["dbg_st"], in_=e["stats"][:, 0:64]), "sdo", 16)
        block.sync(_)
        st_tok = mk.tok("sdo")
    tl_tok, tl_chunk = tload(mk, block, e, AXr, st_tok)

    bT = [av(AXr, c * 1024, 1024, BF16) for c in range(32)]
    bq = [av(AG, c * 1024, 1024, BF16) for c in range(32)]

    def mm_block(wmat, s_w, qtok, is_gate, filler=None, qbuf=None):
        qbuf = bq if qbuf is None else qbuf
        slab_consumed = e.setdefault("wslab_consumed", [0, 0])
        filler_popped = [0]
        filler_n_p1 = 8 if (filler and not is_gate) else 0
        """16 blks x (4 quarters x 8 j2 x 2 rc) matmuls; evac per blk."""
        slab_tok = [0, 0]
        evac_tok = [0, 0]
        sgc = [av(AT, 16384 + 2048, 1024, BF16), av(AT, 16384 + 3072, 1024, BF16)]
        gA = av(AT, 16384 + 4096, 1024, F32)
        sgl_tok = [0, 0]
        for blk in range(16):
            psum = pbig0 if (blk & 1) == 0 else pbig1
            for hj in range(2):
                idx = 2 * blk + hj
                slot = idx & 1
                sem = "sw0" if slot == 0 else "sw1"

                def _(gpsimd, blk=blk, hj=hj, idx=idx, slot=slot, sem=sem):
                    if slab_consumed[slot] > 0:
                        mk.wait(gpsimd, "tt", slab_consumed[slot])
                    mk.inc(gpsimd.dma_start(
                        out=wslab[:, slot, :],
                        in_=wmat[blk, :, 16 * hj:16 * hj + 16, :].rearrange("p a b -> p (a b)")), sem, 16)
                block.gpsimd(_)
                slab_tok[slot] = mk.tok(sem)

                def _(tensor, blk=blk, hj=hj, idx=idx, slot=slot, sem=sem,
                      psum=psum, stok=slab_tok[slot]):
                    mk.wait(tensor, sem, stok)
                    if hj == 0:
                        mk.wait(tensor, "tv", qtok)
                        if evac_tok[blk & 1] > 0:
                            mk.wait(tensor, "tv", evac_tok[blk & 1])
                    for j2 in range(16):
                        j = 16 * hj + j2
                        for rc in range(2):
                            ins = tensor.matmul(
                                psum[:, rc * 512:rc * 512 + 512],
                                wslab[:, slot, j2 * 128:j2 * 128 + 128],
                                qbuf[j][:, rc * 512:rc * 512 + 512],
                                start=(j == 0), stop=(j == 31),
                                skip_group_check=True)
                    mk.inc(ins, "tt", 1)
                block.tensor(_)
                slab_consumed[slot] = mk.tok("tt")
            mm_done = mk.tok("tt")

            if is_gate:
                def _(vector, blk=blk, psum=psum, mm_done=mm_done, sctok=mk.tok("sd0")):
                    mk.wait(vector, "tt", mm_done)
                    mk.wait(vector, "sd0", sctok)
                    vector.scalar_tensor_tensor(out=gA, in0=psum[:], scalar=e["s_swg"][:, blk:blk + 1],
                                                in1=e["scB"][:], op0=ALU.mult, op1=ALU.mult)
                    mk.inc(vector.engine_nop(), "tv", 1)
                block.vector(_)
                evac_tok[blk & 1] = mk.tok("tv")
                ga_tok = mk.tok("tv")
                if filler:
                    want = max(1, (len(filler) + 15 - blk) // (16 - blk))
                    while want > 0 and filler:
                        if filler_popped[0] < filler_n_p1 and filler_popped[0] >= (blk + 1) // 2:
                            break
                        filler.popleft()()
                        filler_popped[0] += 1
                        want -= 1

                def _(scalar, blk=blk, ga_tok=ga_tok):
                    mk.wait(scalar, "tv", ga_tok)
                    if sgl_tok[blk & 1] > 0:
                        mk.wait(scalar, "sdo", sgl_tok[blk & 1])
                    mk.inc(scalar.activation(out=sgc[blk & 1], in_=gA, func=AF.Silu), "tsc", 1)
                block.scalar(_)
                si_tok = mk.tok("tsc")

                def _(sync, blk=blk, si_tok=si_tok):
                    mk.wait(sync, "tsc", si_tok)
                    mk.inc(sync.dma_start(out=e["sg_park"][blk * 128:(blk + 1) * 128, :],
                                          in_=sgc[blk & 1]), "sdo", 16)
                block.sync(_)
                sgl_tok[blk & 1] = mk.tok("sdo")
            else:
                def _(sync, blk=blk):
                    if sgl_tok[blk & 1] > 0:
                        mk.wait(sync, "tv", sgl_tok[blk & 1])
                    mk.inc(sync.dma_start(out=sgc[blk & 1],
                                          in_=e["sg_park"][blk * 128:(blk + 1) * 128, :]), "sd0", 16)
                block.sync(_)
                sgld = mk.tok("sd0")

                def _(vector, blk=blk, psum=psum, mm_done=mm_done, sgld=sgld, sctok=mk.tok("sd0")):
                    mk.wait(vector, "tt", mm_done)
                    mk.wait(vector, "sd0", max(sgld, sctok))
                    vector.scalar_tensor_tensor(out=gA, in0=psum[:], scalar=e["s_swu"][:, blk:blk + 1],
                                                in1=e["scB"][:], op0=ALU.mult, op1=ALU.mult)
                    vector.tensor_tensor(out=gA, in0=gA, in1=sgc[blk & 1], op=ALU.mult)
                    vector.tensor_scalar(out=av(AT, blk * 1024, 1024, BF16), in0=gA,
                                         scalar1=-10.0, scalar2=10.0, op0=ALU.max, op1=ALU.min)
                    mk.inc(vector.engine_nop(), "tv", 1)
                block.vector(_)
                evac_tok[blk & 1] = mk.tok("tv")
                sgl_tok[blk & 1] = mk.tok("tv")
                if filler:
                    want = max(1, (len(filler) + 15 - blk) // (16 - blk))
                    while want > 0 and filler:
                        if filler_popped[0] < filler_n_p1 and filler_popped[0] >= (blk + 1) // 2:
                            break
                        filler.popleft()()
                        filler_popped[0] += 1
                        want -= 1

    from collections import deque
    xs_g = av(AT, 0, 1024, F32)
    xs_g2 = av(AT, 2048, 1024, F32)
    iT = [av(AT, c * 1024, 1024, BF16) for c in range(16)]
    xs_i = av(AG, 0, 1024, F32)
    xs_i2 = av(AG, 2048, 1024, F32)

    qg_units = build_q_units(mk, block, e, bT, bq, e["s_invg"], 32, half_scale=False,
                             xs_tmp=xs_g, row_off=0,
                             deps=[("sdo", st_tok), ("tsc", mk.tok("tsc"))],
                             chunk_sem="sq", chunk_toks=tl_chunk)
    emit_units(qg_units)
    qg_tok = mk.tok("tv")
    emit_scB_load(mk, block, e, row_off=0)
    if e.get("dbg"):
        def _(sync):
            mk.wait(sync, "tv", qg_tok)
            for c in range(32):
                mk.inc(sync.dma_start(out=e["dbg_qg"][c * 128:(c + 1) * 128, :], in_=bq[c]), "sdo", 16)
            mk.inc(sync.dma_start(out=e["dbg_q2"], in_=e["qtmp"][:, 0:3072]), "sdo", 16)
            mk.inc(sync.dma_start(out=e["dbg_q3"], in_=e["qsc"][:]), "sdo", 16)
        block.sync(_)
    qu_units = deque(build_q_units(mk, block, e, bT, bT, e["s_invu"], 32, half_scale=False,
                                   xs_tmp=xs_g, row_off=2048, deps=[("tv", qg_tok)]))
    mm_block(e["wg_t"], e["s_swg"], qg_tok, is_gate=True, filler=qu_units)
    while qu_units:
        qu_units.popleft()()
    qu_tok = mk.tok("tv")
    emit_scB_load(mk, block, e, row_off=2048)
    if hook_after_gate is not None:
        hook_after_gate()
    qi_units = deque(build_q_units(mk, block, e, iT, iT, e["s_invi"], 16, half_scale=True,
                                   xs_tmp=xs_i, row_off=4096, deps=[]))
    mm_block(e["wu_t"], e["s_swu"], qu_tok, is_gate=False, filler=qi_units, qbuf=bT)
    while qi_units:
        qi_units.popleft()()
    qi_tok = mk.tok("tv")
    if e.get("dbg"):
        def _(sync):
            mk.wait(sync, "tv", mk.tok("tv"))
            mk.wait(sync, "tt", mk.tok("tt"))
            for c in range(16):
                mk.inc(sync.dma_start(out=e["dbg_it"][c * 128:(c + 1) * 128, :], in_=iT[c]), "sdo", 16)
        block.sync(_)
    if hook_after_up is not None:
        hook_after_up()

    barrier(mk, block)
    swdB = av(AT, 16384, 4096, F32)
    wdrow = av(AG, 24576, 4096, F32, 1, 0)

    def _(sync):
        mk.inc(sync.dma_start(out=wdrow, in_=e["s_wd"]), "sd0", 16)
    block.sync(_)
    wdr_tok = mk.tok("sd0")

    def _(tensor):
        mk.wait(tensor, "sd0", wdr_tok)
        for h in range(8):
            mk.inc(tensor.matmul(p512[h % 4][:, 0:512] if h < 4 else (pbig0 if h < 6 else pbig1)[:, 512 * (h & 1):512 * (h & 1) + 512],
                                 e["onesr"], wdrow[0:1, 512 * h:512 * h + 512],
                                 start=True, stop=True), "tt", 1)
    block.tensor(_)
    wdb_tok = mk.tok("tt")

    def _(vector):
        mk.wait(vector, "tt", wdb_tok)
        for h in range(8):
            srcp = p512[h % 4][:, 0:512] if h < 4 else (pbig0 if h < 6 else pbig1)[:, 512 * (h & 1):512 * (h & 1) + 512]
            vector.tensor_copy(out=swdB[:, 512 * h:512 * h + 512], in_=srcp)
        mk.inc(vector.engine_nop(), "tv", 1)
    block.vector(_)
    swdB_tok = mk.tok("tv")

    wdsl = [av(AG, 0, 8192, BF16), av(AG, 8192, 8192, BF16)]
    xe = [av(AG, 16384 + k * 1024, 512, F32) for k in range(4)]
    res = [av(AG, 20480 + k * 1024, 512, F32) for k in range(4)]

    def bank(r8):
        if r8 < 4:
            return p512[r8][:, 0:512]
        b = r8 - 4
        return (pbig0 if b < 2 else pbig1)[:, 512 * (b & 1):512 * (b & 1) + 512]

    wd_consumed = [0, 0]
    ev_done = {}
    xe_used = [0, 0, 0, 0]
    res_st = [0, 0, 0, 0]
    for hc in range(8):
        slot = hc & 1
        sem = "sw0" if slot == 0 else "sw1"

        def _(gpsimd, hc=hc, slot=slot, sem=sem):
            if wd_consumed[slot] > 0:
                mk.wait(gpsimd, "tt", wd_consumed[slot])
            if hc == 0:
                mk.wait(gpsimd, "tv", qi_tok)
            mk.inc(gpsimd.dma_start(out=wdsl[slot],
                                    in_=e["wd_t"][hc].rearrange("p a b -> p (a b)")), sem, 16)
        block.gpsimd(_)
        wtok = mk.tok(sem)

        for r8h in range(2):
            def _(tensor, hc=hc, slot=slot, sem=sem, wtok=wtok, r8h=r8h):
                mk.wait(tensor, sem, wtok)
                if hc == 0 and r8h == 0:
                    mk.wait(tensor, "tv", qi_tok)
                    mk.wait(tensor, "tv", swdB_tok)
                for r8 in range(4 * r8h, 4 * r8h + 4):
                    prev = ev_done.get(r8)
                    if prev is not None:
                        mk.wait(tensor, "tv", prev)
                for i in range(16):
                    for r8 in range(4 * r8h, 4 * r8h + 4):
                        ins = tensor.matmul(
                            bank(r8), iT[i][:, r8 * 128:r8 * 128 + 128],
                            wdsl[slot][:, i * 512:i * 512 + 512],
                            start=(i == 0), stop=(i == 15), skip_group_check=True)
                mk.inc(ins, "tt", 1)
            block.tensor(_)
            mmd = mk.tok("tt")
            if r8h == 1:
                wd_consumed[slot] = mmd

            for r8 in range(4 * r8h, 4 * r8h + 4):
                p = (hc * 8 + r8) & 3
                sem2 = "sd0" if (p & 1) == 0 else "sd1"

                def _(sync, hc=hc, r8=r8, p=p, sem2=sem2):
                    if xe_used[p] > 0:
                        mk.wait(sync, "tv", xe_used[p])
                    mk.inc(sync.dma_start(out=xe[p],
                                          in_=src[r8 * 128:(r8 + 1) * 128, hc * 512:(hc + 1) * 512]), sem2, 16)
                block.sync(_)
                xet = mk.tok(sem2)

                def _(vector, hc=hc, r8=r8, p=p, sem2=sem2, xet=xet, mmd=mmd):
                    mk.wait(vector, "tt", mmd)
                    mk.wait(vector, sem2, xet)
                    mk.wait(vector, "tv", swdB_tok)
                    if res_st[p] > 0:
                        mk.wait(vector, "sdo", res_st[p])
                    vector.scalar_tensor_tensor(out=res[p], in0=bank(r8),
                                                scalar=qsc[:, 24 + r8:25 + r8],
                                                in1=swdB[:, hc * 512:(hc + 1) * 512],
                                                op0=ALU.mult, op1=ALU.mult)
                    vector.tensor_tensor(out=res[p], in0=res[p], in1=xe[p], op=ALU.add)
                    if collect_stats:
                        vector.tensor_scalar(out=e["scB"][:, 0:512], in0=res[p], scalar1=1.0,
                                             scalar2=None, op0=ALU.mult, op1=ALU.add,
                                             accum_out=e["pS"][:, 8 * r8 + hc:8 * r8 + hc + 1])
                        vector.scalar_tensor_tensor(out=e["scB"][:, 512:1024], in0=res[p], scalar=1.0,
                                                    in1=res[p], op0=ALU.mult, op1=ALU.mult,
                                                    accum_out=e["pQ"][:, 8 * r8 + hc:8 * r8 + hc + 1])
                    mk.inc(vector.engine_nop(), "tv", 1)
                    xe_used[p] = mk.tok("tv")
                block.vector(_)
                ev_done[r8] = mk.tok("tv")
                evt = mk.tok("tv")

                def _(sync, hc=hc, r8=r8, p=p, evt=evt):
                    mk.wait(sync, "tv", evt)
                    mk.inc(sync.dma_start(out=dst[r8 * 128:(r8 + 1) * 128, hc * 512:(hc + 1) * 512],
                                          in_=res[p]), "sdo", 16)
                block.sync(_)
                res_st[p] = mk.tok("sdo")


def phase_ssm(mk, block, e, src, dst, stats_in=False, collect_stats=False):
    nc = mk.nc
    av, AXr, AG, AT = e["av"], e["AX"], e["AG"], e["AT"]
    p512, pbig0, pbig1 = e["p512"], e["pbig0"], e["pbig1"]
    stats = e["stats"]

    barrier(mk, block)
    st_tok = ln_pass(mk, block, e, src, rev=True, dscale=True, stats_in=stats_in)
    tl_tok, _tlc = tload(mk, block, e, AXr, st_tok)
    uT = [av(AXr, c * 1024, 1024, BF16) for c in range(32)]

    tsl = [[av(AG, (par * 32 + s) * 512, 512, BF16) for s in range(32)] for par in range(2)]
    xe = [av(AT, k * 1024, 512, F32) for k in range(4)]
    du = av(AT, 4096, 512, F32)
    res = [av(AT, 5120 + k * 1024, 512, F32) for k in range(4)]

    def bank(r8):
        if r8 < 4:
            return p512[r8][:, 0:512]
        b = r8 - 4
        return (pbig0 if b < 2 else pbig1)[:, 512 * (b & 1):512 * (b & 1) + 512]

    t_used = [0, 0]
    ev_done = {}
    xe_used = [0, 0, 0, 0]
    res_st = [0, 0, 0, 0]
    for tc in range(8):
        par = tc & 1
        s0 = max(0, 28 - 4 * tc)
        sigmas = list(range(s0, 32))

        def _(scalar, tc=tc, par=par, sigmas=sigmas, s0=s0):
            if t_used[par] > 0:
                mk.wait(scalar, "tt", t_used[par])
            if tc == 0:
                mk.wait(scalar, "sq", tl_tok)
            n = len(sigmas)
            srcap = bass.AP(e["buf3"].tensor, 1 + 128 * s0 + 512 * tc,
                            [[1, 128], [128, n], [1, 512]])
            dst = e["arena"][:, AG + (par * 32 + s0) * 512: AG + (par * 32 + s0) * 512 + n * 512]
            dst = dst.rearrange("p (n t) -> p n t", t=512)
            mk.inc(scalar.dma_start(out=dst, in_=srcap), "sq", 16)
        block.scalar(_)
        tld = mk.tok("sq")

        for r8h in range(2):
            def _(tensor, tc=tc, par=par, sigmas=sigmas, tld=tld, r8h=r8h):
                mk.wait(tensor, "sq", tld)
                for r8 in range(4 * r8h, 4 * r8h + 4):
                    prev = ev_done.get(r8)
                    if prev is not None:
                        mk.wait(tensor, "tv", prev)
                for si, s in enumerate(sigmas):
                    for r8 in range(4 * r8h, 4 * r8h + 4):
                        ins = tensor.matmul(
                            bank(r8), uT[s][:, r8 * 128:r8 * 128 + 128],
                            tsl[par][s],
                            start=(si == 0), stop=(si == len(sigmas) - 1),
                            skip_group_check=True)
                mk.inc(ins, "tt", 1)
            block.tensor(_)
            mmd = mk.tok("tt")
            if r8h == 1:
                t_used[par] = mmd

            for r8 in range(4 * r8h, 4 * r8h + 4):
                p = (tc * 8 + r8) & 3
                sem2 = "sd0" if (p & 1) == 0 else "sd1"

                def _(sync, tc=tc, r8=r8, p=p, sem2=sem2):
                    if xe_used[p] > 0:
                        mk.wait(sync, "tv", xe_used[p])
                    mk.inc(sync.dma_start(out=xe[p],
                                          in_=src[r8 * 128:(r8 + 1) * 128, tc * 512:(tc + 1) * 512]), sem2, 16)
                block.sync(_)
                xet = mk.tok(sem2)

                def _(vector, tc=tc, r8=r8, p=p, sem2=sem2, xet=xet, mmd=mmd):
                    mk.wait(vector, "tt", mmd)
                    mk.wait(vector, sem2, xet)
                    if res_st[p] > 0:
                        mk.wait(vector, "sdo", res_st[p])
                    vector.tensor_scalar(out=du, in0=xe[p], scalar1=stats[:, 40 + r8:41 + r8],
                                         scalar2=stats[:, 48 + r8:49 + r8], op0=ALU.mult, op1=ALU.add)
                    vector.scalar_tensor_tensor(out=res[p], in0=bank(r8), scalar=1.0,
                                                in1=du, op0=ALU.mult, op1=ALU.add)
                    if collect_stats:
                        vector.tensor_scalar(out=e["scB"][:, 0:512], in0=res[p], scalar1=1.0,
                                             scalar2=None, op0=ALU.mult, op1=ALU.add,
                                             accum_out=e["pS"][:, 8 * r8 + tc:8 * r8 + tc + 1])
                        vector.scalar_tensor_tensor(out=e["scB"][:, 512:1024], in0=res[p], scalar=1.0,
                                                    in1=res[p], op0=ALU.mult, op1=ALU.mult,
                                                    accum_out=e["pQ"][:, 8 * r8 + tc:8 * r8 + tc + 1])
                    mk.inc(vector.engine_nop(), "tv", 1)
                    xe_used[p] = mk.tok("tv")
                block.vector(_)
                ev_done[r8] = mk.tok("tv")
                evt = mk.tok("tv")

                def _(sync, tc=tc, r8=r8, p=p, evt=evt):
                    mk.wait(sync, "tv", evt)
                    mk.inc(sync.dma_start(out=dst[r8 * 128:(r8 + 1) * 128, tc * 512:(tc + 1) * 512],
                                          in_=res[p]), "sdo", 16)
                block.sync(_)
                res_st[p] = mk.tok("sdo")


_NC = None

def kernel(**inputs):
    global _NC
    from concourse.bass_utils import run_bass_kernel_spmd
    per_core = host_prep(inputs)
    if _NC is None:
        _NC = build(n_ffn_blocks=2, do_ssm=True, dbg=False)
    res = run_bass_kernel_spmd(_NC, per_core, core_ids=list(range(8)))
    out = np.concatenate([res.results[k]["out"] for k in range(8)], axis=0)
    return out.reshape(2, 4096, 4096)
